# version 09
# baseline (speedup 1.0000x reference)
import math
import numpy as np
import ml_dtypes
import concourse.bass as bass
import concourse.mybir as mybir
from concourse.bass_utils import run_bass_kernel_spmd

F32 = mybir.dt.float32
BF16 = mybir.dt.bfloat16
AF = mybir.ActivationFunctionType
OP = mybir.AluOpType
AX = mybir.AxisListType
BF = ml_dtypes.bfloat16

D = 2048
T = 2048
NH = 16
NE = 8
NEX = 9
I = 1024
TC = 512
NTC = T // TC
DK = D // 128
IC = I // 128
SCALE = 1.0 / math.sqrt(128.0)
EPS = 1e-5

CNT = [12, 16, 28, 32]
MASKSTART = [0, 4, 16, 20]
NMASK = 12

ENGS = ("tensor", "vector", "scalar", "gpsimd", "sync")
_DEBUG_STOP = None
_PARANOID = False


def _kbmap(kb):
    if kb < 8:
        return (0, kb // 4, kb % 4)
    if kb < 24:
        return (1, (kb - 8) // 4, (kb - 8) % 4)
    return (0, (kb - 24) // 4 + 2, (kb - 24) % 4)


class Sem:
    __slots__ = ("h", "n", "_cm")

    def __init__(self, nc, name):
        self._cm = nc.semaphore(name)
        self.h = self._cm.__enter__()
        self.n = 0


class Plan:
    def __init__(self, nc):
        self.nc = nc
        self.q = {e: [] for e in ENGS}
        self.all_sems = []

    def op(self, eng, fname, *args, waits=(), inc=None, inc_k=1, **kw):
        if _PARANOID:
            waits = list(waits) + [(s, s.n) for s in self.all_sems]
        wl = tuple((s.h, v) for (s, v) in waits if v > 0)
        ih = None
        if inc is not None:
            ih = inc.h
            inc.n += inc_k

        def fn(e, _wl=wl, _f=fname, _a=args, _kw=kw, _ih=ih, _ik=inc_k):
            for (h, v) in _wl:
                e.wait_ge(h, v)
            instr = getattr(e, _f)(*_a, **_kw)
            if _ih is not None:
                instr.then_inc(_ih, _ik)

        self.q[eng].append(fn)
        return (inc, inc.n) if inc is not None else None

    def wait(self, eng, *waits):
        wl = tuple((s.h, v) for (s, v) in waits if v > 0)
        if not wl:
            return

        def fn(e, _wl=wl):
            for (h, v) in _wl:
                e.wait_ge(h, v)

        self.q[eng].append(fn)

    def drain(self, eng, inc):
        inc.n += 1

        def fn(e, _ih=inc.h):
            e.drain().then_inc(_ih, 1)

        self.q[eng].append(fn)
        return (inc, inc.n)


def maxw(*pairs):
    d = {}
    for pr in pairs:
        if pr is None:
            continue
        if isinstance(pr, list):
            for q in pr:
                if q is None:
                    continue
                s, v = q
                if v > d.get(s, -1):
                    d[s] = v
            continue
        s, v = pr
        if v > d.get(s, -1):
            d[s] = v
    return [(s, v) for s, v in d.items()]


def _host_prepare(inputs):
    X = np.asarray(inputs["L0_attn_X"])
    f32 = np.float32

    def c(a):
        return np.ascontiguousarray(a, dtype=f32)

    shared = {
        "wq": c(inputs["L0_attn_Wq"]), "wk": c(inputs["L0_attn_Wk"]),
        "wv": c(inputs["L0_attn_Wv"]), "wo": c(inputs["L0_attn_Wo"]),
        "w1t": c(np.transpose(inputs["L0_ffn_W1"], (0, 2, 1))),
        "w3t": c(np.transpose(inputs["L0_ffn_W3"], (0, 2, 1))),
        "w2t": c(np.transpose(inputs["L0_ffn_W2"], (0, 2, 1))),
        "ws1t": c(np.asarray(inputs["L0_ffn_Ws1"]).T),
        "ws3t": c(np.asarray(inputs["L0_ffn_Ws3"]).T),
        "ws2t": c(np.asarray(inputs["L0_ffn_Ws2"]).T),
        "gwt": c(np.asarray(inputs["L0_ffn_gate_w"]).T),
        "gb": c(np.broadcast_to(np.asarray(inputs["L0_ffn_gate_b"], dtype=np.float32).reshape(1, 8), (128, 8))),
        "bv_row": c(np.broadcast_to(np.asarray(inputs["L0_attn_bv"], dtype=np.float32).reshape(1, D), (128, D))),
    }
    for nm, key in [("bq", "L0_attn_bq"), ("bk", "L0_attn_bk"),
                    ("bv", "L0_attn_bv"), ("bo", "L0_attn_bo")]:
        shared[nm] = c(np.asarray(inputs[key], dtype=f32).reshape(DK, 128).T)
    shared["ones_stats"] = np.full((128, 128), 1.0 / D, dtype=BF)
    shared["onesrow_f32"] = np.ones((128, 128), dtype=BF)
    shared["ones128"] = np.ones((128, 128), dtype=BF)
    oh8 = np.zeros((128, 8 * 128), dtype=BF)
    for e in range(8):
        oh8[e, e * 128:(e + 1) * 128] = 1.0
    shared["onehotrow8"] = oh8
    shared["ident_f32"] = np.eye(128, dtype=BF)

    in_maps = []
    tok_idx = {}
    for core in range(8):
        b, role = core // 2, core % 2
        idx = np.r_[0:1024, 3072:4096] if role == 0 else np.r_[1024:3072]
        tok_idx[core] = (b, idx)
        x0 = np.ascontiguousarray(X[b][idx].T, dtype=f32)
        mask = np.zeros((4, NMASK, 128, TC), dtype=BF)
        qglob = [[0, 1, 6, 7], [2, 3, 4, 5]][role]
        for s in range(4):
            qi = qglob[s]
            for m in range(NMASK):
                kb = MASKSTART[s] + m
                key = 128 * kb + np.arange(128)[:, None]
                qpos = TC * qi + np.arange(TC)[None, :]
                mask[s, m] = (key <= qpos).astype(BF)
        md = {"x0": x0, "maskseq": np.ascontiguousarray(mask.reshape(4 * NMASK * 128, TC))}
        md.update(shared)
        in_maps.append(md)
    return in_maps, tok_idx


def _build(debug_stop):
    nc = bass.Bass("TRN2", target_bir_lowering=False, dynamic_dma_scratch_size=4096)
    p = Plan(nc)

    ap = {}
    def din(name, shape, dt=F32):
        ap[name] = nc.dram_tensor(name, shape, dt, kind="ExternalInput").ap()
    din("x0", [D, T]); din("maskseq", [4 * NMASK * 128, TC], BF16)
    din("wq", [D, D]); din("wk", [D, D]); din("wv", [D, D]); din("wo", [D, D])
    din("w1t", [NE, D, I]); din("w3t", [NE, D, I]); din("w2t", [NE, I, D])
    din("ws1t", [D, I]); din("ws3t", [D, I]); din("ws2t", [I, D])
    din("gwt", [D, 8]); din("gb", [128, 8])
    for nm in ("bq", "bk", "bv", "bo"):
        din(nm, [128, DK])
    din("bv_row", [128, D])
    din("ones_stats", [128, 128], BF16); din("onesrow_f32", [128, 128], BF16)
    din("ones128", [128, 128], BF16); din("onehotrow8", [128, 8 * 128], BF16)
    din("ident_f32", [128, 128], BF16)

    out_x3 = nc.dram_tensor("x3", [D, T], F32, kind="ExternalOutput").ap()

    x1d = nc.dram_tensor("x1d", [D, T], F32).ap()
    lnbd = nc.dram_tensor("lnbd", [D, T], BF16).ap()
    x2d = nc.dram_tensor("x2d", [D, T], F32).ap()
    ln3d = nc.dram_tensor("ln3d", [NTC * D, TC], BF16).ap()
    qTd = nc.dram_tensor("qTd", [NTC * D, TC], BF16).ap()
    kcd = nc.dram_tensor("kcd", [NTC * D, TC], BF16).ap()
    vcd = nc.dram_tensor("vcd", [T, D], BF16).ap()
    gkd = nc.dram_tensor("gkd", [2 * NTC * D, TC], BF16).ap()
    gvd = nc.dram_tensor("gvd", [2 * T, D], BF16).ap()
    w1c = nc.dram_tensor("w1c", [NEX, D, I], BF16).ap()
    w3c = nc.dram_tensor("w3c", [NEX, D, I], BF16).ap()
    w2c = nc.dram_tensor("w2c", [DK, NEX * I, 128], BF16).ap()
    wqkvc = {"wq": nc.dram_tensor("wqc", [D, D], BF16).ap(),
             "wk": nc.dram_tensor("wkc", [D, D], BF16).ap(),
             "wv": nc.dram_tensor("wvc", [D, D], BF16).ap(),
             "wo": nc.dram_tensor("woc", [D, D], BF16).ap()}

    _alive = []
    def sbuf(name, shape, dt):
        t = nc.sbuf_tensor(name, shape, dt)
        h = t.__enter__()
        _alive.append(t)
        return h

    c_ones_stats = sbuf("c_ones_stats", [128, 128], BF16)
    c_onesrow = sbuf("c_onesrow", [128, 128], BF16)
    c_ones128 = sbuf("c_ones128", [128, 128], BF16)
    c_oh8 = sbuf("c_oh8", [128, 8 * 128], BF16)
    c_ident = sbuf("c_ident", [128, 128], BF16)
    c_eps = sbuf("c_eps", [128, 1], F32)
    gwt_sb = sbuf("gwt_sb", [128, DK * 8], BF16)
    gwtf_sb = sbuf("gwtf_sb", [128, DK * 8], F32)
    gbrep = sbuf("gbrep", [128, 8], F32)
    NCST = 2
    WCST = 1024
    cst_f = sbuf("cst_f", [128, NCST * WCST], F32)
    cst_b = sbuf("cst_b", [128, NCST * WCST], BF16)
    bias_sb = {nm: sbuf(nm + "_sb", [128, DK], F32) for nm in ("bq", "bk", "bv", "bo")}
    dwT = sbuf("dwT", [8, T], BF16)
    mean_rep = sbuf("mean_rep", [128, TC], F32)
    msq_sb = sbuf("msq_sb", [128, TC], F32)
    var_sb = sbuf("var_sb", [128, TC], F32)
    lnv_sb = msq_sb
    rstd_sb = var_sb
    gate_t = {}
    for nm, w in [("lg", 8), ("m1", 1), ("sc", 8), ("bi", 8), ("d1", 8),
                  ("mk1", 8), ("b2", 8), ("m2", 1), ("rc", 1), ("mk2", 8)]:
        gate_t[nm] = sbuf("g_" + nm, [128, w], F32)
    gate_t["dwf"] = sbuf("g_dwf", [128, 8], BF16)

    def psum(name):
        t = nc.psum_tensor(name, [128, TC], F32)
        h = t.__enter__()
        _alive.append(t)
        return h
    B = [psum(f"pb{i}") for i in range(8)]
    bank_free = [None] * 8

    s_pe = Sem(nc, "s_pe")
    s_act = Sem(nc, "s_act")
    s_dve = Sem(nc, "s_dve")
    s_syq = Sem(nc, "s_syq")
    s_gpq = Sem(nc, "s_gpq")
    s_cc = Sem(nc, "s_cc")
    s_scq = Sem(nc, "s_scq")
    s_out = Sem(nc, "s_out")
    p.all_sems = [s_pe, s_act, s_dve, s_syq, s_gpq, s_cc, s_out]

    for nm, dst in [("ones_stats", c_ones_stats), ("onesrow_f32", c_onesrow),
                    ("ones128", c_ones128), ("onehotrow8", c_oh8), ("ident_f32", c_ident),
                    ("gb", gbrep)]:
        p.op("sync", "dma_start", inc=s_syq, inc_k=16, out=dst[:, :], in_=ap[nm])
    for nm in ("bq", "bk", "bv", "bo"):
        p.op("sync", "dma_start", inc=s_syq, inc_k=16, out=bias_sb[nm][:, :], in_=ap[nm])
    p.op("sync", "dma_start", inc=s_syq, inc_k=16,
         out=gwtf_sb[:, :].rearrange("p (k g) -> p k g", k=DK),
         in_=ap["gwt"].rearrange("(k p) g -> p k g", p=128))
    p.op("vector", "memset", c_eps[:, :], EPS, inc=s_dve)
    p.op("vector", "tensor_copy", gwt_sb[:, :], gwtf_sb[:, :],
         waits=[(s_syq, s_syq.n)], inc=s_dve)
    pre_w = [(s_syq, s_syq.n), (s_dve, s_dve.n)]
    PRE = {"syq": (s_syq, s_syq.n), "dve": (s_dve, s_dve.n)}

    cast_state = {"st_free": [None] * NCST, "out_free": [None] * NCST, "u": 0}
    s_cin = [Sem(nc, f"s_cin{u}") for u in range(NCST)]
    s_cout = [Sem(nc, f"s_cout{u}") for u in range(NCST)]

    def cast_piece(src_f32_ap, dst_bf16_ap):
        u = cast_state["u"] % NCST
        cast_state["u"] += 1
        ncols = src_f32_ap.shape[-1]
        stg = cst_f[:, u * WCST:u * WCST + ncols]
        out = cst_b[:, u * WCST:u * WCST + ncols]
        p.op("sync", "dma_start", waits=maxw(cast_state["st_free"][u]),
             inc=s_cin[u], inc_k=16, out=stg, in_=src_f32_ap)
        p.op("vector", "tensor_copy", out, stg,
             waits=maxw((s_cin[u], s_cin[u].n), cast_state["out_free"][u]),
             inc=s_dve)
        dr = p.drain("vector", s_dve)
        cast_state["st_free"][u] = dr
        p.op("scalar", "dma_start", waits=[dr], inc=s_cout[u], inc_k=16,
             out=dst_bf16_ap, in_=out)
        cast_state["out_free"][u] = (s_cout[u], s_cout[u].n)

    cache_ready = {}
    cast_jobs = []
    for e in range(NEX):
        for wcache, wsrc, wssrc in [(w1c, ap["w1t"], ap["ws1t"]),
                                    (w3c, ap["w3t"], ap["ws3t"])]:
            src = (wsrc[e] if e < NE else wssrc).rearrange("(k p) i -> p k i", p=128)
            dstc = (wcache[e] if e < NE else wcache[8]).rearrange("(k p) i -> p k i", p=128)
            for k in range(DK):
                cast_jobs.append(("piece", src[:, k, :], dstc[:, k, :]))
        cast_jobs.append(("mark", ("w13", e)))
    w2r = w2c.rearrange("do (t p) c -> p do t c", p=128)
    for dq in range(4):
        for e in range(NEX):
            src = (ap["w2t"][e] if e < NE else ap["ws2t"]).rearrange("(j p) d -> p j d", p=128)
            for j in range(IC):
                cast_jobs.append(("piece", src[:, j, dq * 512:(dq + 1) * 512],
                                  w2r[:, dq * 4:(dq + 1) * 4, e * IC + j]))
        cast_jobs.append(("mark", ("w2", dq)))
    for wname in ("wq", "wk", "wv", "wo"):
        src = ap[wname].rearrange("(k p) d -> p k d", p=128)
        dstc = wqkvc[wname].rearrange("(k p) d -> p k d", p=128)
        for k in range(DK):
            for dh in range(2):
                cast_jobs.append(("piece", src[:, k, dh * 1024:(dh + 1) * 1024],
                                  dstc[:, k, dh * 1024:(dh + 1) * 1024]))
        cast_jobs.append(("mark", (wname,)))
    cast_jobs.reverse()

    def feed_casts(upto=None, n=None):
        if upto is not None and upto in cache_ready:
            return
        cnt = 0
        while cast_jobs:
            if n is not None and cnt >= n:
                return
            job = cast_jobs.pop()
            if job[0] == "mark":
                cache_ready[job[1]] = [(s, s.n) for s in s_cout]
                if upto is not None and job[1] == upto:
                    return
            else:
                cast_piece(job[1], job[2])
                cnt += 1

    PRE = {"dve": (s_dve, s_dve.n), "syq": (s_syq, s_syq.n), "gpq": (s_gpq, s_gpq.n)}
    ln_guard = {"stats": None, "pass2": None}

    def emit_ln(xin, xin_wait, ln_dst, want_gate, gate_tc, lnp, extra_p2_wait=None):
        xb_sb, tn_sb = lnp["xb"], lnp["tn"]
        xq_sb = xb_sb
        for dk in range(DK):
            sl_ = dk % 2
            xb = xb_sb[:, sl_ * TC:(sl_ + 1) * TC]
            wl = maxw(lnp["xb_free"][sl_], xin_wait if dk == 0 else None)
            p.op("scalar", "copy", out=xb, in_=xin[:, dk * TC:(dk + 1) * TC],
                 waits=wl, inc=s_act)
            cv = (s_act, s_act.n)
            wl = [cv]
            if dk == 0:
                wl = maxw(cv, bank_free[6])
            p.op("tensor", "matmul", B[6][:, :], c_ones_stats[:, :], xb,
                 start=(dk == 0), stop=(dk == DK - 1), waits=wl, inc=s_pe)
            lnp["xb_free"][sl_] = (s_pe, s_pe.n)
        for dk in range(DK):
            sl_ = dk % 2
            xq = xq_sb[:, sl_ * TC:(sl_ + 1) * TC]
            p.op("scalar", "activation", xq, xin[:, dk * TC:(dk + 1) * TC], AF.Square,
                 waits=maxw(lnp["xb_free"][sl_]), inc=s_act)
            qv = (s_act, s_act.n)
            wl = [qv]
            if dk == 0:
                wl = maxw(qv, bank_free[7])
            p.op("tensor", "matmul", B[7][:, :], c_ones_stats[:, :], xq,
                 start=(dk == 0), stop=(dk == DK - 1), waits=wl, inc=s_pe)
            lnp["xb_free"][sl_] = (s_pe, s_pe.n)
        rep_mm = (s_pe, s_pe.n)
        p.op("scalar", "copy", out=mean_rep[:, :], in_=B[6][:, :],
             waits=maxw(rep_mm, ln_guard["pass2"]), inc=s_act)
        mr = (s_act, s_act.n)
        p.op("vector", "tensor_tensor", msq_sb[:, :], mean_rep[:, :], mean_rep[:, :],
             op=OP.mult, waits=[mr], inc=s_dve)
        p.op("vector", "tensor_tensor", var_sb[:, :], B[7][:, :], msq_sb[:, :],
             op=OP.subtract, waits=[(s_dve, s_dve.n)], inc=s_dve)
        bank_free[7] = (s_dve, s_dve.n)
        p.op("scalar", "activation", lnv_sb[:, :], var_sb[:, :], AF.Ln,
             bias=c_eps[:, 0:1], waits=[(s_dve, s_dve.n)], inc=s_act)
        p.op("scalar", "activation", rstd_sb[:, :], lnv_sb[:, :], AF.Exp, scale=-0.5,
             waits=[(s_act, s_act.n)], inc=s_act)
        rstd_done = (s_act, s_act.n)
        bank_free[6] = mr
        for dk in range(DK):
            tn = tn_sb[:, :]
            wl = maxw(rstd_done, lnp["tn_free"][0], extra_p2_wait,
                      xin_wait if dk == 0 else None)
            p.op("vector", "tensor_tensor", tn, xin[:, dk * TC:(dk + 1) * TC],
                 mean_rep[:, :], op=OP.subtract, waits=wl, inc=s_dve)
            p.op("vector", "tensor_tensor", ln_dst[:, dk * TC:(dk + 1) * TC],
                 tn, rstd_sb[:, :], op=OP.mult, waits=[(s_dve, s_dve.n)], inc=s_dve)
            lnp["tn_free"][0] = (s_dve, s_dve.n)
        ln_guard["pass2"] = (s_dve, s_dve.n)
        ln_fill = (s_dve, s_dve.n)
        if not want_gate:
            return ln_fill
        g = gate_t
        for tt in range(4):
            for dk in range(DK):
                wl = []
                if dk == 0:
                    wl = maxw(ln_fill, bank_free[6])
                p.op("tensor", "matmul", B[6][0:128, 0:8],
                     ln_dst[:, dk * TC + tt * 128: dk * TC + tt * 128 + 128],
                     gwt_sb[:, dk * 8:(dk + 1) * 8],
                     start=(dk == 0), stop=(dk == DK - 1), waits=wl,
                     inc=(s_pe if dk == DK - 1 else None))
            gmm = (s_pe, s_pe.n)
            p.op("vector", "tensor_copy", g["lg"][:, :], B[6][0:128, 0:8],
                 waits=[gmm], inc=s_dve)
            bank_free[6] = (s_dve, s_dve.n)
            p.op("vector", "reduce_max", g["m1"][:, :], g["lg"][:, :], axis=AX.X,
                 waits=[(s_dve, s_dve.n)], inc=s_dve)
            p.op("vector", "tensor_scalar", g["d1"][:, :], g["lg"][:, :],
                 g["m1"][:, 0:1], None, op0=OP.subtract,
                 waits=[(s_dve, s_dve.n)], inc=s_dve)
            p.op("scalar", "activation", g["sc"][:, :], g["d1"][:, :], AF.Exp,
                 waits=[(s_dve, s_dve.n)], inc=s_act)
            p.op("vector", "reduce_sum", g["m1"][:, :], g["sc"][:, :], axis=AX.X,
                 waits=[(s_act, s_act.n)], inc=s_dve)
            p.op("vector", "reciprocal", g["rc"][:, 0:1], g["m1"][:, 0:1],
                 waits=[(s_dve, s_dve.n)], inc=s_dve)
            for f, a, kw in [
                ("tensor_scalar", (g["sc"][:, :], g["sc"][:, :], g["rc"][:, 0:1], None), dict(op0=OP.mult)),
                ("tensor_tensor", (g["bi"][:, :], g["sc"][:, :], gbrep[:, :]), dict(op=OP.add)),
                ("reduce_max", (g["m1"][:, :], g["bi"][:, :]), dict(axis=AX.X)),
                ("tensor_scalar", (g["d1"][:, :], g["bi"][:, :], g["m1"][:, 0:1], None), dict(op0=OP.subtract)),
                ("tensor_single_scalar", (g["mk1"][:, :], g["d1"][:, :], 0.0), dict(op=OP.is_ge)),
                ("scalar_tensor_tensor", (g["b2"][:, :], g["mk1"][:, :], -1e30, g["bi"][:, :]), dict(op0=OP.mult, op1=OP.add)),
                ("reduce_max", (g["m2"][:, :], g["b2"][:, :]), dict(axis=AX.X)),
                ("tensor_scalar", (g["d1"][:, :], g["b2"][:, :], g["m2"][:, 0:1], None), dict(op0=OP.subtract)),
                ("tensor_single_scalar", (g["mk2"][:, :], g["d1"][:, :], 0.0), dict(op=OP.is_ge)),
                ("tensor_tensor", (g["mk1"][:, :], g["mk1"][:, :], g["mk2"][:, :]), dict(op=OP.add)),
                ("scalar_tensor_tensor", (g["dwf"][:, :], g["sc"][:, :], 0.5, g["mk1"][:, :]), dict(op0=OP.mult, op1=OP.mult)),
            ]:
                p.op("vector", f, *a, waits=[(s_dve, s_dve.n)], inc=s_dve, **kw)
            p.op("tensor", "matmul", B[7][0:8, 0:128], g["dwf"][:, :], c_ident[:, :],
                 start=True, stop=True,
                 waits=maxw((s_dve, s_dve.n), bank_free[7]), inc=s_pe)
            p.op("vector", "tensor_copy",
                 dwT[0:8, gate_tc * TC + tt * 128: gate_tc * TC + tt * 128 + 128],
                 B[7][0:8, 0:128], waits=[(s_pe, s_pe.n)], inc=s_dve)
            bank_free[7] = (s_dve, s_dve.n)
        return ln_fill

    def mk_lnp(ssb, tag):
        return {
            "xb": ssb("xb" + tag, [128, 2 * TC], BF16),
            "tn": ssb("tn" + tag, [128, TC], F32),
            "xb_free": [None] * 2, "tn_free": [None],
        }

    cache_state = {}

    def emit_ffn(inv, x_in, x_out, fence, x_in_store):
        sfx = f"_f{inv}"
        scope = []
        def ssb(name, shape, dt):
            t = nc.sbuf_tensor(name + sfx, shape, dt)
            h = t.__enter__()
            scope.append(t)
            return h

        ln_sb = ssb("ln", [128, 2 * DK * TC], BF16)
        hs_sb = ssb("hs", [128, NEX * IC * TC], BF16)
        w13_sb = ssb("w13", [128, 3 * DK * 256], BF16)
        w2_sb = ssb("w2", [128, 2 * 36 * 128], BF16)
        silu_sb = ssb("silu", [128, TC], BF16)
        hmul_sb = ssb("hmul", [128, TC], BF16)
        xin_sb = ssb("xin", [128, DK * TC], F32)
        xo_sb = ssb("xo", [128, TC], F32)
        lnp = mk_lnp(ssb, sfx)
        fl = fence if isinstance(fence, list) else [fence]
        xin_free = fl
        w13_free = [fl] * 3
        w2_free = [fl] * 2
        xo_free = [fl]
        ln_free = [fl, fl]
        hs_free = fl
        silu_free = [fl]
        use = {"w13": 0, "w2": 0, "xo": 0}
        for k in ("xb_free", "tn_free"):
            lnp[k] = [fl] * len(lnp[k])

        out_sem = s_out if (x_out is out_x3) else s_scq
        out_store_val = [None] * NTC

        for tc in range(NTC):
            lslot = tc % 2
            lnd = ln_sb[:, lslot * DK * TC:(lslot + 1) * DK * TC]
            wl = maxw(xin_free)
            if x_in_store is not None:
                wl = maxw(*wl, x_in_store[tc])
            p.op("sync", "dma_start", waits=wl, inc=s_syq, inc_k=16,
                 out=xin_sb[:, :].rearrange("p (k t) -> p k t", k=DK),
                 in_=x_in.rearrange("(k p) t -> p k t", p=128)[:, :, tc * TC:(tc + 1) * TC])
            xin_fill = (s_syq, s_syq.n)
            if inv == 0:
                lf = emit_ln(xin_sb[:, :], xin_fill, lnd, debug_stop != "ln1", tc,
                             lnp, extra_p2_wait=ln_free[lslot])
                dw_ready = (s_dve, s_dve.n)
                if debug_stop == "stats" and tc == 0:
                    p.op("gpsimd", "dma_start", waits=maxw(lf), inc=s_out, inc_k=16,
                         out=out_x3.rearrange("(k p) t -> p k t", p=128)[:, 0, 0:TC],
                         in_=mean_rep[:, :])
                    p.op("gpsimd", "dma_start", waits=maxw(lf), inc=s_out, inc_k=16,
                         out=out_x3.rearrange("(k p) t -> p k t", p=128)[:, 1, 0:TC],
                         in_=rstd_sb[:, :])
                    p.op("gpsimd", "dma_start", waits=maxw(lf), inc=s_out, inc_k=16,
                         out=out_x3.rearrange("(k p) t -> p k t", p=128)[:, 2, 0:TC],
                         in_=lnv_sb[:, :])
                    p.op("gpsimd", "dma_start", waits=maxw(lf), inc=s_out, inc_k=16,
                         out=out_x3.rearrange("(k p) t -> p k t", p=128)[:, 3, 0:TC],
                         in_=lnd[:, 0:TC])
                if debug_stop == "ln1":
                    lnrel = p.drain("vector", s_dve)
                    if False:
                        for j, src in enumerate([mean_rep[:, :], rstd_sb[:, :],
                                                 msq_sb[:, :]]):
                            p.op("gpsimd", "dma_start", waits=maxw(lf, lnrel),
                                 inc=s_out, inc_k=16,
                                 out=out_x3.rearrange("(k p) t -> p k t", p=128)[:, j, tc * TC:(tc + 1) * TC],
                                 in_=src)
                        p.op("gpsimd", "dma_start", waits=maxw(lf, lnrel),
                             inc=s_out, inc_k=16,
                             out=out_x3.rearrange("(k p) t -> p k t", p=128)[:, 3, tc * TC:tc * TC + 128],
                             in_=c_ones_stats[:, :])
                    else:
                        p.op("sync", "dma_start", waits=maxw(lf, lnrel), inc=s_syq, inc_k=16,
                             out=lnbd.rearrange("(k p) t -> p k t", p=128)[:, :, tc * TC:(tc + 1) * TC],
                             in_=lnd.rearrange("p (k t) -> p k t", k=DK))
                    ln_free[lslot] = [(s_syq, s_syq.n)]
                    if tc == NTC - 1:
                        p.op("gpsimd", "dma_start", waits=[(s_syq, s_syq.n)],
                             inc=s_out, inc_k=16, out=out_x3, in_=lnbd)
            else:
                wl = maxw(ln_free[lslot], fence)
                p.op("sync", "dma_start", waits=wl, inc=s_syq, inc_k=16,
                     out=lnd.rearrange("p (k t) -> p k t", k=DK),
                     in_=ln3d.rearrange("(c k p) t -> p c k t", c=NTC, p=128)[:, tc])
                lf = (s_syq, s_syq.n)
                dw_ready = None
            ln_ready = lf
            if debug_stop in ("ln1", "stats"):
                xin_free = [p.drain("vector", s_dve)]
                continue

            dwrep_ready = {}
            for e in range(NEX):
                if inv == 0:
                    if tc == 0:
                        feed_casts(upto=("w13", e))
                    else:
                        feed_casts(n=6)
                if e < NE:
                    dbank = 4 + (e % 2)
                    p.op("tensor", "matmul", B[dbank][:, :],
                         c_oh8[0:8, e * 128:(e + 1) * 128], dwT[0:8, tc * TC:(tc + 1) * TC],
                         start=True, stop=True,
                         waits=maxw(dw_ready, bank_free[dbank], PRE["syq"],
                                    fence if inv == 1 else None), inc=s_pe)
                    dwrep_ready[e] = (s_pe, s_pe.n)
                for iq in range(4):
                    wslots = []
                    for wi, wcache in enumerate([w1c, w3c]):
                        s = use["w13"] % 3
                        use["w13"] += 1
                        dst = w13_sb[:, s * DK * 256:(s + 1) * DK * 256]
                        dst3 = dst.rearrange("p (k i) -> p k i", k=DK)
                        srcc = (wcache[e] if e < NE else wcache[8]).rearrange(
                            "(k p) i -> p k i", p=128)[:, :, iq * 256:(iq + 1) * 256]
                        wl = maxw(w13_free[s], cache_ready[("w13", e)])
                        p.op("sync", "dma_start", waits=wl, inc=s_syq, inc_k=16,
                             out=dst3, in_=srcc)
                        fillv = (s_syq, s_syq.n)
                        wslots.append((s, dst, fillv, None))
                    for ici in range(2):
                        ic = iq * 2 + ici
                        pa, pb = (0, 1) if (ic % 2 == 0) else (2, 3)
                        for (bank, (s, dst, fillv, wb)) in zip((pa, pb), wslots):
                            for dk in range(DK):
                                wl = []
                                if dk == 0:
                                    wl = maxw(fillv, bank_free[bank], ln_ready)
                                p.op("tensor", "matmul", B[bank][:, :],
                                     dst[:, dk * 256 + ici * 128: dk * 256 + ici * 128 + 128],
                                     lnd[:, dk * TC:(dk + 1) * TC],
                                     start=(dk == 0), stop=(dk == DK - 1), waits=wl,
                                     inc=(s_pe if dk == DK - 1 else None))
                            if ici == 1:
                                w13_free[s] = [(s_pe, s_pe.n), wb] if wb else [(s_pe, s_pe.n)]
                        mmv = (s_pe, s_pe.n)
                        sil = silu_sb[:, :]
                        hm = hmul_sb[:, :]
                        p.op("scalar", "activation", sil, B[pa][:, :], AF.Silu,
                             waits=maxw(mmv, silu_free[0]), inc=s_act)
                        bank_free[pa] = (s_act, s_act.n)
                        p.op("vector", "tensor_tensor", hm, sil, B[pb][:, :], op=OP.mult,
                             waits=[(s_act, s_act.n)], inc=s_dve)
                        bank_free[pb] = (s_dve, s_dve.n)
                        silu_free[0] = (s_dve, s_dve.n)
                        hdst = hs_sb[:, (e * IC + ic) * TC:(e * IC + ic + 1) * TC]
                        if e < NE:
                            dbank = 4 + (e % 2)
                            p.op("vector", "tensor_tensor", hdst, hm, B[dbank][:, :],
                                 op=OP.mult,
                                 waits=maxw(dwrep_ready[e], hs_free), inc=s_dve)
                            if ic == IC - 1:
                                bank_free[dbank] = (s_dve, s_dve.n)
                        else:
                            p.op("vector", "tensor_scalar", hdst, hm, 0.5, None,
                                 op0=OP.mult, waits=maxw(hs_free), inc=s_dve)
            hs_done = (s_dve, s_dve.n)
            ln_free[lslot] = [(s_pe, s_pe.n)]

            for dout in range(DK):
                if inv == 0 and tc == 0:
                    feed_casts(upto=("w2", dout // 4))
                hA, hB = 0, 1
                dstA = w2_sb[:, hA * 36 * 128:(hA + 1) * 36 * 128]
                dstB = w2_sb[:, hB * 36 * 128:(hB + 1) * 36 * 128]
                w2cr = w2c[dout].rearrange("(j p) c -> p j c", p=128)
                p.op("sync", "dma_start",
                     waits=maxw(w2_free[hA], cache_ready[("w2", dout // 4)]),
                     inc=s_syq, inc_k=16,
                     out=dstA.rearrange("p (j c) -> p j c", c=128), in_=w2cr[:, 0:36])
                fillA = (s_syq, s_syq.n)
                p.op("sync", "dma_start", waits=maxw(w2_free[hB]),
                     inc=s_syq, inc_k=16,
                     out=dstB.rearrange("p (j c) -> p j c", c=128), in_=w2cr[:, 36:72])
                fillB = (s_syq, s_syq.n)
                xr = xin_sb[:, dout * TC:(dout + 1) * TC]
                xr_fill = xin_fill
                bank = 2 + (dout % 2)
                for nmm in range(NEX * IC):
                    wl = []
                    if nmm == 0:
                        wl = maxw(fillA, hs_done, bank_free[bank])
                    elif nmm == 36:
                        wl = [fillB]
                    src_t = (dstA[:, nmm * 128:(nmm + 1) * 128] if nmm < 36
                             else dstB[:, (nmm - 36) * 128:(nmm - 35) * 128])
                    p.op("tensor", "matmul", B[bank][:, :], src_t,
                         hs_sb[:, nmm * TC:(nmm + 1) * TC],
                         start=(nmm == 0), stop=(nmm == NEX * IC - 1), waits=wl,
                         inc=(s_pe if nmm in (35, NEX * IC - 1) else None))
                    if nmm == 35:
                        w2_free[hA] = [(s_pe, s_pe.n)]
                ymm = (s_pe, s_pe.n)
                w2_free[hB] = [ymm]
                os_ = 0
                xo = xo_sb[:, :]
                p.op("vector", "tensor_tensor", xo, B[bank][:, :], xr, op=OP.add,
                     waits=maxw(ymm, xr_fill, xo_free[os_]), inc=s_dve)
                bank_free[bank] = (s_dve, s_dve.n)
                xod = p.drain("vector", s_dve)
                p.op("scalar", "dma_start", waits=[xod],
                     inc=out_sem, inc_k=16,
                     out=x_out.rearrange("(k p) t -> p k t", p=128)[:, dout, tc * TC:(tc + 1) * TC],
                     in_=xo)
                xo_free[os_] = [(out_sem, out_sem.n)]
            hs_free = [(s_pe, s_pe.n)]
            xin_free = [(s_dve, s_dve.n)]
            out_store_val[tc] = (out_sem, out_sem.n)
        if inv == 0:
            feed_casts()
        for t in reversed(scope):
            t.__exit__(None, None, None)
        end = maxw((s_pe, s_pe.n), (s_dve, s_dve.n), (s_act, s_act.n),
                   (s_syq, s_syq.n), (s_scq, s_scq.n), (out_sem, out_sem.n))
        return out_store_val, end

    def emit_attn(x_in, x_out, fence, x1_store):
        scope = []
        def ssb(name, shape, dt):
            t = nc.sbuf_tensor(name + "_at", shape, dt)
            h = t.__enter__()
            scope.append(t)
            return h

        wo_sb = ssb("wo", [128, DK * D], BF16)
        lnp = mk_lnp(ssb, "_at")
        scope1 = []
        def ssb1(name, shape, dt):
            t = nc.sbuf_tensor(name + "_at", shape, dt)
            h = t.__enter__()
            scope1.append(t)
            return h
        wbuf_sb = ssb1("wbuf", [128, 2 * DK * TC], BF16)
        ln2_sb = ssb1("ln2", [128, DK * TC], BF16)
        stage_sb = ssb1("stage", [128, 16 * TC], BF16)
        xin_sb = ssb1("xina", [128, DK * TC], F32)
        bvrep = ssb1("bvrep", [128, D], F32)

        fl = fence if isinstance(fence, list) else [fence]
        for k in ("xb_free", "tn_free"):
            lnp[k] = [fl] * len(lnp[k])
        xin_free = fl
        w_free = [fl] * 2
        use = {"w": 0, "q": 0, "k": 0, "v": 0, "p": 0, "ot": 0, "xra": 0}
        out_sem = s_out if (x_out is out_x3) else s_scq

        p.op("sync", "dma_start", waits=maxw(cache_ready[("wo",)], *fl),
             inc=s_syq, inc_k=16,
             out=wo_sb[:, :].rearrange("p (k d) -> p k d", k=DK),
             in_=wqkvc["wo"].rearrange("(k p) d -> p k d", p=128))
        wo_ready = (s_syq, s_syq.n)
        p.op("sync", "dma_start", waits=maxw(fl), inc=s_syq, inc_k=16,
             out=bvrep[:, :], in_=ap["bv_row"])
        bvrep_ready = (s_syq, s_syq.n)

        ag_done = {}
        ln2_free = None
        qkv_cache_wb = None
        stage_guard = None
        for tc in range(NTC):
            p.op("sync", "dma_start",
                 waits=maxw(xin_free, x1_store[tc], *fl), inc=s_syq, inc_k=16,
                 out=xin_sb[:, :].rearrange("p (k t) -> p k t", k=DK),
                 in_=x_in.rearrange("(k p) t -> p k t", p=128)[:, :, tc * TC:(tc + 1) * TC])
            xin_fill = (s_syq, s_syq.n)
            lf = emit_ln(xin_sb[:, :], xin_fill, ln2_sb[:, :], False, tc, lnp,
                         extra_p2_wait=ln2_free)
            xin_free = [p.drain("vector", s_dve)]
            ln2_ready = lf
            kst = None
            for wi, wname in enumerate(("wq", "wk")):
                for oc in range(4):
                    ws = use["w"] % 2
                    use["w"] += 1
                    wdst = wbuf_sb[:, ws * DK * TC:(ws + 1) * DK * TC]
                    wdst3 = wdst.rearrange("p (k c) -> p k c", k=DK)
                    p.op("sync", "dma_start",
                         waits=maxw(w_free[ws], cache_ready[(wname,)], *fl),
                         inc=s_syq, inc_k=16, out=wdst3,
                         in_=wqkvc[wname].rearrange("(k p) c -> p k c", p=128)[:, :, oc * TC:(oc + 1) * TC])
                    wfill = (s_syq, s_syq.n)
                    wwb = None
                    for hc in range(4):
                        bank = hc % 2
                        for dk in range(DK):
                            wl = []
                            if dk == 0:
                                wl = maxw(wfill, bank_free[bank], ln2_ready)
                            p.op("tensor", "matmul", B[bank][:, :],
                                 wdst[:, dk * TC + hc * 128: dk * TC + hc * 128 + 128],
                                 ln2_sb[:, dk * TC:(dk + 1) * TC],
                                 start=(dk == 0), stop=(dk == DK - 1), waits=wl,
                                 inc=(s_pe if dk == DK - 1 else None))
                        mmv = (s_pe, s_pe.n)
                        if hc == 3:
                            w_free[ws] = [mmv, wwb] if wwb else [mmv]
                        hglob = oc * 4 + hc
                        p.op("vector", "tensor_scalar",
                             stage_sb[:, hglob * TC:(hglob + 1) * TC], B[bank][:, :],
                             bias_sb["bq" if wi == 0 else "bk"][:, hglob:hglob + 1], None,
                             op0=OP.add, waits=maxw(mmv, stage_guard), inc=s_dve)
                        bank_free[bank] = (s_dve, s_dve.n)
                stg = p.drain("vector", s_dve)
                dest = qTd if wi == 0 else kcd
                p.op("scalar", "dma_start", waits=[stg], inc=s_scq, inc_k=16,
                     out=dest.rearrange("(c h p) t -> p c h t", c=NTC, p=128)[:, tc],
                     in_=stage_sb[:, :].rearrange("p (h t) -> p h t", h=16))
                stage_guard = (s_scq, s_scq.n)
                if wi == 1:
                    kst = stage_guard
            for oc in range(4):
                ws = use["w"] % 2
                use["w"] += 1
                wdst = wbuf_sb[:, ws * DK * TC:(ws + 1) * DK * TC]
                wdst3 = wdst.rearrange("p (k c) -> p k c", k=DK)
                p.op("sync", "dma_start",
                     waits=maxw(w_free[ws], cache_ready[("wv",)], *fl),
                     inc=s_syq, inc_k=16, out=wdst3,
                     in_=wqkvc["wv"].rearrange("(k p) c -> p k c", p=128)[:, :, oc * TC:(oc + 1) * TC])
                wfill = (s_syq, s_syq.n)
                wwb = None
                for tt in range(4):
                    bank = tt % 2
                    for dk in range(DK):
                        wl = []
                        if dk == 0:
                            wl = maxw(wfill, bank_free[bank], ln2_ready)
                        p.op("tensor", "matmul", B[bank][:, :],
                             ln2_sb[:, dk * TC + tt * 128: dk * TC + tt * 128 + 128],
                             wdst[:, dk * TC:(dk + 1) * TC],
                             start=(dk == 0), stop=(dk == DK - 1), waits=wl,
                             inc=(s_pe if dk == DK - 1 else None))
                    mmv = (s_pe, s_pe.n)
                    if tt == 3:
                        w_free[ws] = [mmv, wwb] if wwb else [mmv]
                    p.op("vector", "tensor_tensor",
                         stage_sb[:, (tt * 4 + oc) * TC:(tt * 4 + oc + 1) * TC],
                         B[bank][:, :], bvrep[:, oc * TC:(oc + 1) * TC], op=OP.add,
                         waits=maxw(mmv, stage_guard, bvrep_ready), inc=s_dve)
                    bank_free[bank] = (s_dve, s_dve.n)
            stg = p.drain("vector", s_dve)
            if debug_stop == "den" and tc == 0:
                p.op("vector", "tensor_copy", xin_sb[:, 0:TC], stage_sb[:, 0:TC],
                     waits=[stg], inc=s_dve)
                dnr4 = p.drain("vector", s_dve)
                p.op("scalar", "dma_start", waits=[dnr4], inc=s_out, inc_k=16,
                     out=out_x3.rearrange("(k p) t -> p k t", p=128)[:, 14, TC:2 * TC],
                     in_=xin_sb[:, 0:TC])
            p.op("scalar", "dma_start", waits=[stg], inc=s_scq, inc_k=16,
                 out=vcd[tc * TC:(tc + 1) * TC, :].rearrange("(tt p) c -> p tt c", p=128),
                 in_=stage_sb[:, :].rearrange("p (tt c) -> p tt c", tt=4))
            vst = (s_scq, s_scq.n)
            stage_guard = vst
            ln2_free = (s_pe, s_pe.n)
            p.wait("gpsimd", kst)
            p.op("gpsimd", "collective_compute", "AllGather", OP.bypass,
                 replica_groups=[[0, 1], [2, 3], [4, 5], [6, 7]],
                 ins=[kcd[tc * D:(tc + 1) * D, :].opt()],
                 outs=[gkd[tc * 2 * D:(tc + 1) * 2 * D, :].opt()],
                 inc=s_cc)
            p.wait("gpsimd", vst)
            p.op("gpsimd", "collective_compute", "AllGather", OP.bypass,
                 replica_groups=[[0, 1], [2, 3], [4, 5], [6, 7]],
                 ins=[vcd[tc * TC:(tc + 1) * TC, :].opt()],
                 outs=[gvd[tc * 2 * TC:(tc + 1) * 2 * TC, :].opt()],
                 inc=s_cc)
            ag_done[tc] = (s_cc, s_cc.n)
        qT_store = stage_guard
        for t in reversed(scope1):
            t.__exit__(None, None, None)
        qkv_end = maxw((s_pe, s_pe.n), (s_dve, s_dve.n), (s_act, s_act.n),
                       (s_syq, s_syq.n), (s_gpq, s_gpq.n))
        fl2 = qkv_end
        if debug_stop == "vdump":
            dmb = ssb("dmb", [128, D], BF16)
            dmf = ssb("dmf", [128, D], F32)
            Q = maxw(*fl2, ag_done[NTC - 1], (s_scq, s_scq.n))
            ldfree = None
            cpfree = None
            for blk in range(16):
                p.op("sync", "dma_start", waits=maxw(*Q, ldfree),
                     inc=s_syq, inc_k=16, out=dmb[:, :],
                     in_=gvd[blk * 128:(blk + 1) * 128, :])
                p.op("vector", "tensor_copy", dmf[:, :], dmb[:, :],
                     waits=maxw((s_syq, s_syq.n), cpfree), inc=s_dve)
                ldfree = p.drain("vector", s_dve)
                p.op("scalar", "dma_start", waits=[ldfree], inc=s_out, inc_k=16,
                     out=out_x3.rearrange("(k p) t -> p k t", p=128)[:, blk],
                     in_=dmf[:, :])
                cpfree = (s_out, s_out.n)
            for t in reversed(scope):
                t.__exit__(None, None, None)
            return [None] * NTC, maxw((s_out, s_out.n))
        msk_sb = ssb("msk", [128, NMASK * TC], BF16)
        q_sb = ssb("qs", [128, 2 * TC], BF16)
        kbuf_sb = ssb("kbuf", [128, 3 * TC], BF16)
        vbuf_sb = ssb("vbuf", [128, 3 * TC], BF16)
        pt_sb = ssb("pt", [128, 5 * TC], BF16)
        lnden_sb = ssb("lnden", [128, 2 * TC], F32)
        attnN_sb = ssb("attnN", [128, NH * TC], BF16)
        x2st_sb = ssb("x2st", [128, DK * TC], F32)
        ot_sb = ssb("ot", [128, TC], F32)
        xra_sb = ssb("xra", [128, 2 * TC], F32)
        ln3_sb = ssb("ln3", [128, DK * TC], BF16)
        q_free = [fl2] * 2
        k_free = [fl2] * 3
        v_free = [fl2] * 3
        p_free = [fl2] * 5
        ot_free = [fl2] * 2
        xra_free = [fl2] * 2

        x2_store = [None] * NTC
        msk_free = None
        lnden_guard = [None, None]
        attnN_guard = None
        x2st_guard = [None] * DK
        x2st_fill = [None] * DK
        ln3_guard = None
        for sl in range(4):
            p.op("sync", "dma_start", waits=maxw(msk_free, *fl2), inc=s_syq, inc_k=16,
                 out=msk_sb[:, :].rearrange("p (m t) -> p m t", m=NMASK),
                 in_=ap["maskseq"].rearrange("(s m p) t -> p s m t", s=4, p=128)[:, sl])
            msk_fill = (s_syq, s_syq.n)
            cnt = CNT[sl]
            ag_need = ag_done[1] if cnt <= 16 else ag_done[3]
            for h in range(NH):
                qs = use["q"] % 2
                use["q"] += 1
                qt = q_sb[:, qs * TC:(qs + 1) * TC]
                p.op("sync", "dma_start", waits=maxw(q_free[qs], qT_store),
                     inc=s_syq, inc_k=16, out=qt,
                     in_=qTd.rearrange("(c hh p) t -> p c hh t", c=NTC, p=128)[:, sl, h])
                q_fill = (s_syq, s_syq.n)
                ngrp = cnt // 4
                kg_fill = [None] * ngrp
                vg_fill = [None] * ngrp
                kg_slot = [0] * ngrp
                vg_slot = [0] * ngrp

                def load_group(gidx):
                    r, c, _ = _kbmap(gidx * 4)
                    ks = use["k"] % 3
                    use["k"] += 1
                    kg_slot[gidx] = ks
                    p.op("sync", "dma_start", waits=maxw(k_free[ks], ag_need),
                         inc=s_syq, inc_k=16,
                         out=kbuf_sb[:, ks * TC:(ks + 1) * TC],
                         in_=gkd[c * 2 * D + r * D + h * 128: c * 2 * D + r * D + (h + 1) * 128, :])
                    kg_fill[gidx] = (s_syq, s_syq.n)
                    vs = use["v"] % 3
                    use["v"] += 1
                    vg_slot[gidx] = vs
                    p.op("sync", "dma_start", waits=maxw(v_free[vs], ag_need),
                         inc=s_syq, inc_k=16,
                         out=vbuf_sb[:, vs * TC:(vs + 1) * TC].rearrange("p (tt c) -> p tt c", tt=4),
                         in_=gvd[c * 2 * TC + r * TC:c * 2 * TC + (r + 1) * TC,
                                 h * 128:(h + 1) * 128].rearrange("(tt p) c -> p tt c", p=128))
                    vg_fill[gidx] = (s_syq, s_syq.n)

                for gidx in range(min(3, ngrp)):
                    load_group(gidx)
                abank = 2 + (h % 2)
                dbank = 4 + (h % 2)
                SHIFT = 3
                ptm = [None] * cnt
                last_score = None
                for step in range(cnt + SHIFT):
                    kb = step
                    if kb < cnt:
                        g_ = kb // 4
                        o = kb % 4
                        sbank = kb % 2
                        wl = maxw(kg_fill[g_], q_fill if kb == 0 else None,
                                  bank_free[sbank])
                        p.op("tensor", "matmul", B[sbank][:, :],
                             kbuf_sb[:, kg_slot[g_] * TC + o * 128: kg_slot[g_] * TC + o * 128 + 128],
                             qt, start=True, stop=True, waits=wl, inc=s_pe)
                        smm = (s_pe, s_pe.n)
                        if kb == cnt - 1:
                            last_score = smm
                        ps = use["p"] % 5
                        use["p"] += 1
                        pt = pt_sb[:, ps * TC:(ps + 1) * TC]
                        p.op("scalar", "activation", pt, B[sbank][:, :], AF.Exp,
                             scale=SCALE, waits=maxw(smm, p_free[ps]), inc=s_act)
                        bank_free[sbank] = (s_act, s_act.n)
                        pv = (s_act, s_act.n)
                        if MASKSTART[sl] <= kb < MASKSTART[sl] + NMASK:
                            mi = kb - MASKSTART[sl]
                            p.op("vector", "tensor_tensor", pt, pt,
                                 msk_sb[:, mi * TC:(mi + 1) * TC], op=OP.mult,
                                 waits=maxw(pv, msk_fill), inc=s_dve)
                            pv = (s_dve, s_dve.n)
                        ptm[kb] = (ps, pt, pv)
                    kc = step - SHIFT
                    if kc >= 0:
                        g2 = kc // 4
                        o2 = kc % 4
                        ps2, pt2, pv2 = ptm[kc]
                        wl = maxw(pv2, vg_fill[g2],
                                  bank_free[abank] if kc == 0 else None)
                        p.op("tensor", "matmul", B[abank][:, :],
                             vbuf_sb[:, vg_slot[g2] * TC + o2 * 128: vg_slot[g2] * TC + o2 * 128 + 128],
                             pt2, start=(kc == 0), stop=(kc == cnt - 1), waits=wl,
                             inc=(s_pe if kc == cnt - 1 else None))
                        wl = maxw(bank_free[dbank] if kc == 0 else None,
                                  PRE["syq"] if kc == 0 else None)
                        p.op("tensor", "matmul", B[dbank][:, :], c_ones128[:, :],
                             pt2, start=(kc == 0), stop=(kc == cnt - 1), waits=wl,
                             inc=s_pe)
                        pcons = (s_pe, s_pe.n)
                        p_free[ps2] = [pcons]
                        if o2 == 3 or kc == cnt - 1:
                            k_free[kg_slot[g2]] = [pcons]
                            v_free[vg_slot[g2]] = [pcons]
                            if g2 + 3 < ngrp:
                                load_group(g2 + 3)
                q_free[qs] = [last_score]
                den_mm = (s_pe, s_pe.n)
                ls = h % 2
                p.op("scalar", "activation", lnden_sb[:, ls * TC:(ls + 1) * TC],
                     B[dbank][:, :], AF.Ln, waits=maxw(den_mm, lnden_guard[ls]), inc=s_act)
                bank_free[dbank] = (s_act, s_act.n)
                p.op("scalar", "activation", lnden_sb[:, ls * TC:(ls + 1) * TC],
                     lnden_sb[:, ls * TC:(ls + 1) * TC], AF.Exp, scale=-1.0,
                     waits=[(s_act, s_act.n)], inc=s_act)
                rv = (s_act, s_act.n)
                p.op("vector", "tensor_tensor", attnN_sb[:, h * TC:(h + 1) * TC],
                     B[abank][:, :], lnden_sb[:, ls * TC:(ls + 1) * TC], op=OP.mult,
                     waits=maxw(rv, attnN_guard), inc=s_dve)
                if debug_stop == "den" and sl == 1 and h < 4:
                    p.op("vector", "tensor_copy", ot_sb[:, 0:TC], qt,
                         waits=[(s_dve, s_dve.n)], inc=s_dve)
                    dnr = p.drain("vector", s_dve)
                    p.op("scalar", "dma_start", waits=[dnr], inc=s_out, inc_k=16,
                         out=out_x3.rearrange("(k p) t -> p k t", p=128)[:, 8 + h, 0:TC],
                         in_=ot_sb[:, 0:TC])
                    ot_free[0] = [(s_out, s_out.n)]
                bank_free[abank] = (s_dve, s_dve.n)
                lnden_guard[ls] = (s_dve, s_dve.n)
            attn_done = (s_dve, s_dve.n)
            msk_free = p.drain("vector", s_dve)
            if debug_stop == "ndump" and sl == 1:
                QD = maxw(attn_done, msk_free, (s_pe, s_pe.n), (s_act, s_act.n),
                          (s_syq, s_syq.n), (s_scq, s_scq.n), (s_cc, s_cc.n),
                          (out_sem, out_sem.n), (s_out, s_out.n))
                p.op("vector", "tensor_copy", x2st_sb[:, :], attnN_sb[:, :],
                     waits=QD, inc=s_dve)
                dnr = p.drain("vector", s_dve)
                p.op("scalar", "dma_start", waits=[dnr], inc=s_out, inc_k=16,
                     out=out_x3.rearrange("(k p) t -> p k t", p=128)[:, :, 0:TC],
                     in_=x2st_sb[:, :].rearrange("p (k t) -> p k t", k=DK))
                for t in reversed(scope):
                    t.__exit__(None, None, None)
                return [None] * NTC, maxw((s_out, s_out.n))
            if debug_stop == "den" and sl == 1:
                for hh in range(4):
                    p.op("vector", "tensor_copy", ot_sb[:, 0:TC],
                         attnN_sb[:, hh * TC:(hh + 1) * TC],
                         waits=maxw(attn_done, ot_free[0]), inc=s_dve)
                    dnr2 = p.drain("vector", s_dve)
                    p.op("scalar", "dma_start", waits=[dnr2], inc=s_out, inc_k=16,
                         out=out_x3.rearrange("(k p) t -> p k t", p=128)[:, hh, 0:TC],
                         in_=ot_sb[:, 0:TC])
                    ot_free[0] = [(s_out, s_out.n)]
            for dout in range(DK):
                bank = 6 + (dout % 2)
                rs = use["xra"] % 2
                use["xra"] += 1
                xr = xra_sb[:, rs * TC:(rs + 1) * TC]
                p.op("sync", "dma_start", waits=maxw(xra_free[rs], x1_store[sl]),
                     inc=s_syq, inc_k=16, out=xr,
                     in_=x_in.rearrange("(k p) t -> p k t", p=128)[:, dout, sl * TC:(sl + 1) * TC])
                xr_fill = (s_syq, s_syq.n)
                for dk in range(DK):
                    wl = []
                    if dk == 0:
                        wl = maxw(attn_done, wo_ready, bank_free[bank])
                    p.op("tensor", "matmul", B[bank][:, :],
                         wo_sb[:, dk * D + dout * 128: dk * D + dout * 128 + 128],
                         attnN_sb[:, dk * TC:(dk + 1) * TC],
                         start=(dk == 0), stop=(dk == DK - 1), waits=wl,
                         inc=(s_pe if dk == DK - 1 else None))
                omm = (s_pe, s_pe.n)
                ot = ot_sb[:, 0:TC]
                p.op("vector", "tensor_scalar", ot, B[bank][:, :],
                     bias_sb["bo"][:, dout:dout + 1], None, op0=OP.add,
                     waits=maxw(omm, ot_free[0]), inc=s_dve)
                bank_free[bank] = (s_dve, s_dve.n)
                p.op("vector", "tensor_tensor",
                     x2st_sb[:, dout * TC:(dout + 1) * TC], ot, xr, op=OP.add,
                     waits=maxw((s_dve, s_dve.n), xr_fill, x2st_guard[dout]), inc=s_dve)
                ot_free[0] = [(s_dve, s_dve.n)]
                xrel = p.drain("vector", s_dve)
                xra_free[rs] = [xrel]
                x2st_fill[dout] = xrel
            attnN_guard = (s_pe, s_pe.n)
            p.op("scalar", "dma_start", waits=maxw(*x2st_fill),
                 inc=out_sem, inc_k=16,
                 out=x_out.rearrange("(k p) t -> p k t", p=128)[:, :, sl * TC:(sl + 1) * TC],
                 in_=x2st_sb[:, :].rearrange("p (k t) -> p k t", k=DK))
            x2_store[sl] = (out_sem, out_sem.n)
            x2_store_pair = (out_sem, out_sem.n)

            lf3 = emit_ln(x2st_sb[:, :], maxw(*x2st_fill), ln3_sb[:, :], True, sl,
                          lnp, extra_p2_wait=ln3_guard)
            for dk in range(DK):
                x2st_guard[dk] = [ln_guard["pass2"], x2_store_pair]
            ln3rel = p.drain("vector", s_dve)
            p.op("scalar", "dma_start", waits=maxw(lf3, ln3rel),
                 inc=s_scq, inc_k=16,
                 out=ln3d.rearrange("(c k p) t -> p c k t", c=NTC, p=128)[:, sl],
                 in_=ln3_sb[:, :].rearrange("p (k t) -> p k t", k=DK))
            ln3_guard = [(s_pe, s_pe.n), (s_scq, s_scq.n)]
        for t in reversed(scope):
            t.__exit__(None, None, None)
        end = maxw((s_pe, s_pe.n), (s_dve, s_dve.n), (s_act, s_act.n),
                   (s_syq, s_syq.n), (s_scq, s_scq.n), (s_cc, s_cc.n),
                   (out_sem, out_sem.n))
        return x2_store, end

    fence0 = maxw(*pre_w, (s_pe, s_pe.n), (s_dve, s_dve.n))
    if debug_stop == "xraw":
        for dk in range(DK):
            p.op("sync", "dma_start", inc=s_out, inc_k=16,
                 out=out_x3.rearrange("(k p) t -> p k t", p=128)[:, dk],
                 in_=ap["x0"].rearrange("(k p) t -> p k t", p=128)[:, dk])
    elif debug_stop in ("ffn1", "ln1", "stats"):
        emit_ffn(0, ap["x0"], out_x3, fence0, None)
    elif debug_stop in ("attn", "den", "vdump", "ndump"):
        store1, end1 = emit_ffn(0, ap["x0"], x1d, fence0, None)
        emit_attn(x1d, out_x3, end1, store1)
    else:
        store1, end1 = emit_ffn(0, ap["x0"], x1d, fence0, None)
        x2s, end2 = emit_attn(x1d, x2d, end1, store1)
        emit_ffn(1, x2d, out_x3, end2, x2s)
    p.wait("sync", (s_out, s_out.n))

    with nc.Block() as block:
        for eng in ENGS:
            fns = p.q[eng]
            if not fns:
                continue
            def body(e, _fns=tuple(fns)):
                for f in _fns:
                    f(e)
            getattr(block, eng)(body)
    return nc


_BUILD_CACHE = {}


def kernel(**inputs):
    in_maps, tok_idx = _host_prepare(inputs)
    key = (_DEBUG_STOP, _PARANOID)
    if key not in _BUILD_CACHE:
        _BUILD_CACHE[key] = _build(_DEBUG_STOP)
    nc = _BUILD_CACHE[key]
    res = run_bass_kernel_spmd(nc, in_maps, core_ids=list(range(8)))
    out = np.zeros((4, 4096, 2048), dtype=np.float32)
    for core in range(8):
        b, idx = tok_idx[core]
        out[b, idx, :] = np.asarray(res.results[core]["x3"]).T
    return out


# version 10
# speedup vs baseline: 1.0676x; 1.0676x over previous
import math
import numpy as np
import ml_dtypes
import concourse.bass as bass
import concourse.mybir as mybir
from concourse.bass_utils import run_bass_kernel_spmd

F32 = mybir.dt.float32
BF16 = mybir.dt.bfloat16
AF = mybir.ActivationFunctionType
OP = mybir.AluOpType
AX = mybir.AxisListType
BF = ml_dtypes.bfloat16

D = 2048
T = 2048
NH = 16
NE = 8
NEX = 9
I = 1024
TC = 512
NTC = T // TC
DK = D // 128
IC = I // 128
SCALE = 1.0 / math.sqrt(128.0)
EPS = 1e-5

CNT = [12, 16, 28, 32]
MASKSTART = [0, 4, 16, 20]
NMASK = 12

ENGS = ("tensor", "vector", "scalar", "gpsimd", "sync")
_DEBUG_STOP = None
_PARANOID = False


def _kbmap(kb):
    if kb < 8:
        return (0, kb // 4, kb % 4)
    if kb < 24:
        return (1, (kb - 8) // 4, (kb - 8) % 4)
    return (0, (kb - 24) // 4 + 2, (kb - 24) % 4)


class Sem:
    __slots__ = ("h", "n", "_cm")

    def __init__(self, nc, name):
        self._cm = nc.semaphore(name)
        self.h = self._cm.__enter__()
        self.n = 0


class Plan:
    def __init__(self, nc):
        self.nc = nc
        self.q = {e: [] for e in ENGS}
        self.all_sems = []

    def op(self, eng, fname, *args, waits=(), inc=None, inc_k=1, **kw):
        if _PARANOID:
            waits = list(waits) + [(s, s.n) for s in self.all_sems]
        wl = tuple((s.h, v) for (s, v) in waits if v > 0)
        ih = None
        if inc is not None:
            ih = inc.h
            inc.n += inc_k

        def fn(e, _wl=wl, _f=fname, _a=args, _kw=kw, _ih=ih, _ik=inc_k):
            for (h, v) in _wl:
                e.wait_ge(h, v)
            instr = getattr(e, _f)(*_a, **_kw)
            if _ih is not None:
                instr.then_inc(_ih, _ik)

        self.q[eng].append(fn)
        return (inc, inc.n) if inc is not None else None

    def wait(self, eng, *waits):
        wl = tuple((s.h, v) for (s, v) in waits if v > 0)
        if not wl:
            return

        def fn(e, _wl=wl):
            for (h, v) in _wl:
                e.wait_ge(h, v)

        self.q[eng].append(fn)

    def drain(self, eng, inc):
        inc.n += 1

        def fn(e, _ih=inc.h):
            e.drain().then_inc(_ih, 1)

        self.q[eng].append(fn)
        return (inc, inc.n)


def maxw(*pairs):
    d = {}
    for pr in pairs:
        if pr is None:
            continue
        if isinstance(pr, list):
            for q in pr:
                if q is None:
                    continue
                s, v = q
                if v > d.get(s, -1):
                    d[s] = v
            continue
        s, v = pr
        if v > d.get(s, -1):
            d[s] = v
    return [(s, v) for s, v in d.items()]


def _host_prepare(inputs):
    X = np.asarray(inputs["L0_attn_X"])
    f32 = np.float32

    def c(a):
        return np.ascontiguousarray(a, dtype=f32)

    shared = {
        "wq": c(inputs["L0_attn_Wq"]), "wk": c(inputs["L0_attn_Wk"]),
        "wv": c(inputs["L0_attn_Wv"]), "wo": c(inputs["L0_attn_Wo"]),
        "w1t": c(np.transpose(inputs["L0_ffn_W1"], (0, 2, 1))),
        "w3t": c(np.transpose(inputs["L0_ffn_W3"], (0, 2, 1))),
        "w2t": c(np.transpose(inputs["L0_ffn_W2"], (0, 2, 1))),
        "ws1t": c(np.asarray(inputs["L0_ffn_Ws1"]).T),
        "ws3t": c(np.asarray(inputs["L0_ffn_Ws3"]).T),
        "ws2t": c(np.asarray(inputs["L0_ffn_Ws2"]).T),
        "gwt": c(np.asarray(inputs["L0_ffn_gate_w"]).T),
        "gb": c(np.broadcast_to(np.asarray(inputs["L0_ffn_gate_b"], dtype=np.float32).reshape(1, 8), (128, 8))),
        "bv_row": c(np.broadcast_to(np.asarray(inputs["L0_attn_bv"], dtype=np.float32).reshape(1, D), (128, D))),
    }
    for nm, key in [("bq", "L0_attn_bq"), ("bk", "L0_attn_bk"),
                    ("bv", "L0_attn_bv"), ("bo", "L0_attn_bo")]:
        shared[nm] = c(np.asarray(inputs[key], dtype=f32).reshape(DK, 128).T)
    shared["ones_stats"] = np.full((128, 128), 1.0 / D, dtype=BF)
    shared["onesrow_f32"] = np.ones((128, 128), dtype=BF)
    shared["ones128"] = np.ones((128, 128), dtype=BF)
    oh8 = np.zeros((128, 8 * 128), dtype=BF)
    for e in range(8):
        oh8[e, e * 128:(e + 1) * 128] = 1.0
    shared["onehotrow8"] = oh8
    shared["ident_f32"] = np.eye(128, dtype=BF)

    in_maps = []
    tok_idx = {}
    for core in range(8):
        b, role = core // 2, core % 2
        idx = np.r_[0:1024, 3072:4096] if role == 0 else np.r_[1024:3072]
        tok_idx[core] = (b, idx)
        x0 = np.ascontiguousarray(X[b][idx].T, dtype=f32)
        mask = np.zeros((4, NMASK, 128, TC), dtype=BF)
        qglob = [[0, 1, 6, 7], [2, 3, 4, 5]][role]
        for s in range(4):
            qi = qglob[s]
            for m in range(NMASK):
                kb = MASKSTART[s] + m
                key = 128 * kb + np.arange(128)[:, None]
                qpos = TC * qi + np.arange(TC)[None, :]
                mask[s, m] = (key <= qpos).astype(BF)
        md = {"x0": x0, "maskseq": np.ascontiguousarray(mask.reshape(4 * NMASK * 128, TC))}
        md.update(shared)
        in_maps.append(md)
    return in_maps, tok_idx


def _build(debug_stop):
    nc = bass.Bass("TRN2", target_bir_lowering=False, dynamic_dma_scratch_size=4096)
    p = Plan(nc)

    ap = {}
    def din(name, shape, dt=F32):
        ap[name] = nc.dram_tensor(name, shape, dt, kind="ExternalInput").ap()
    din("x0", [D, T]); din("maskseq", [4 * NMASK * 128, TC], BF16)
    din("wq", [D, D]); din("wk", [D, D]); din("wv", [D, D]); din("wo", [D, D])
    din("w1t", [NE, D, I]); din("w3t", [NE, D, I]); din("w2t", [NE, I, D])
    din("ws1t", [D, I]); din("ws3t", [D, I]); din("ws2t", [I, D])
    din("gwt", [D, 8]); din("gb", [128, 8])
    for nm in ("bq", "bk", "bv", "bo"):
        din(nm, [128, DK])
    din("bv_row", [128, D])
    din("ones_stats", [128, 128], BF16); din("onesrow_f32", [128, 128], BF16)
    din("ones128", [128, 128], BF16); din("onehotrow8", [128, 8 * 128], BF16)
    din("ident_f32", [128, 128], BF16)

    out_x3 = nc.dram_tensor("x3", [D, T], F32, kind="ExternalOutput").ap()

    x1d = nc.dram_tensor("x1d", [D, T], F32).ap()
    lnbd = nc.dram_tensor("lnbd", [D, T], BF16).ap()
    x2d = nc.dram_tensor("x2d", [D, T], F32).ap()
    ln3d = nc.dram_tensor("ln3d", [NTC * D, TC], BF16).ap()
    qTd = nc.dram_tensor("qTd", [NTC * D, TC], BF16).ap()
    kcd = nc.dram_tensor("kcd", [NTC * D, TC], BF16).ap()
    vcd = nc.dram_tensor("vcd", [T, D], BF16).ap()
    gkd = nc.dram_tensor("gkd", [2 * NTC * D, TC], BF16).ap()
    gvd = nc.dram_tensor("gvd", [2 * T, D], BF16).ap()
    w1c = nc.dram_tensor("w1c", [NEX, D, I], BF16).ap()
    w3c = nc.dram_tensor("w3c", [NEX, D, I], BF16).ap()
    w2c = nc.dram_tensor("w2c", [DK, NEX * I, 128], BF16).ap()
    wqkvc = {"wq": nc.dram_tensor("wqc", [D, D], BF16).ap(),
             "wk": nc.dram_tensor("wkc", [D, D], BF16).ap(),
             "wv": nc.dram_tensor("wvc", [D, D], BF16).ap(),
             "wo": nc.dram_tensor("woc", [D, D], BF16).ap()}

    _alive = []
    def sbuf(name, shape, dt):
        t = nc.sbuf_tensor(name, shape, dt)
        h = t.__enter__()
        _alive.append(t)
        return h

    c_ones_stats = sbuf("c_ones_stats", [128, 128], BF16)
    c_onesrow = sbuf("c_onesrow", [128, 128], BF16)
    c_ones128 = sbuf("c_ones128", [128, 128], BF16)
    c_oh8 = sbuf("c_oh8", [128, 8 * 128], BF16)
    c_ident = sbuf("c_ident", [128, 128], BF16)
    c_eps = sbuf("c_eps", [128, 1], F32)
    gwt_sb = sbuf("gwt_sb", [128, DK * 8], BF16)
    gwtf_sb = sbuf("gwtf_sb", [128, DK * 8], F32)
    gbrep = sbuf("gbrep", [128, 8], F32)
    NCST = 2
    WCST = 1024
    cst_f = sbuf("cst_f", [128, NCST * WCST], F32)
    cst_b = sbuf("cst_b", [128, NCST * WCST], BF16)
    bias_sb = {nm: sbuf(nm + "_sb", [128, DK], F32) for nm in ("bq", "bk", "bv", "bo")}
    dwT = sbuf("dwT", [8, T], BF16)
    mean_rep = sbuf("mean_rep", [128, TC], F32)
    msq_sb = sbuf("msq_sb", [128, TC], F32)
    var_sb = sbuf("var_sb", [128, TC], F32)
    lnv_sb = msq_sb
    rstd_sb = var_sb
    gate_t = {}
    for nm, w in [("lg", 8), ("m1", 1), ("sc", 8), ("bi", 8), ("d1", 8),
                  ("mk1", 8), ("b2", 8), ("m2", 1), ("rc", 1), ("mk2", 8)]:
        gate_t[nm] = sbuf("g_" + nm, [128, w], F32)
    gate_t["dwf"] = sbuf("g_dwf", [128, 8], BF16)

    def psum(name):
        t = nc.psum_tensor(name, [128, TC], F32)
        h = t.__enter__()
        _alive.append(t)
        return h
    B = [psum(f"pb{i}") for i in range(8)]
    bank_free = [None] * 8

    s_pe = Sem(nc, "s_pe")
    s_act = Sem(nc, "s_act")
    s_dve = Sem(nc, "s_dve")
    s_syq = Sem(nc, "s_syq")
    s_gpq = Sem(nc, "s_gpq")
    s_cc = Sem(nc, "s_cc")
    s_scq = Sem(nc, "s_scq")
    s_out = Sem(nc, "s_out")
    p.all_sems = [s_pe, s_act, s_dve, s_syq, s_gpq, s_cc, s_out]

    for nm, dst in [("ones_stats", c_ones_stats), ("onesrow_f32", c_onesrow),
                    ("ones128", c_ones128), ("onehotrow8", c_oh8), ("ident_f32", c_ident),
                    ("gb", gbrep)]:
        p.op("sync", "dma_start", inc=s_syq, inc_k=16, out=dst[:, :], in_=ap[nm])
    for nm in ("bq", "bk", "bv", "bo"):
        p.op("sync", "dma_start", inc=s_syq, inc_k=16, out=bias_sb[nm][:, :], in_=ap[nm])
    p.op("sync", "dma_start", inc=s_syq, inc_k=16,
         out=gwtf_sb[:, :].rearrange("p (k g) -> p k g", k=DK),
         in_=ap["gwt"].rearrange("(k p) g -> p k g", p=128))
    p.op("vector", "memset", c_eps[:, :], EPS, inc=s_dve)
    p.op("vector", "tensor_copy", gwt_sb[:, :], gwtf_sb[:, :],
         waits=[(s_syq, s_syq.n)], inc=s_dve)
    pre_w = [(s_syq, s_syq.n), (s_dve, s_dve.n)]
    PRE = {"syq": (s_syq, s_syq.n), "dve": (s_dve, s_dve.n)}

    cast_state = {"st_free": [None] * NCST, "out_free": [None] * NCST, "u": 0}
    s_cin = [Sem(nc, f"s_cin{u}") for u in range(NCST)]
    s_cout = [Sem(nc, f"s_cout{u}") for u in range(NCST)]

    def cast_piece(src_f32_ap, dst_bf16_ap):
        u = cast_state["u"] % NCST
        cast_state["u"] += 1
        ncols = src_f32_ap.shape[-1]
        stg = cst_f[:, u * WCST:u * WCST + ncols]
        out = cst_b[:, u * WCST:u * WCST + ncols]
        p.op("sync", "dma_start", waits=maxw(cast_state["st_free"][u]),
             inc=s_cin[u], inc_k=16, out=stg, in_=src_f32_ap)
        p.op("vector", "tensor_copy", out, stg,
             waits=maxw((s_cin[u], s_cin[u].n), cast_state["out_free"][u]),
             inc=s_dve)
        dr = p.drain("vector", s_dve)
        cast_state["st_free"][u] = dr
        p.op("scalar", "dma_start", waits=[dr], inc=s_cout[u], inc_k=16,
             out=dst_bf16_ap, in_=out)
        cast_state["out_free"][u] = (s_cout[u], s_cout[u].n)

    cache_ready = {}
    cast_jobs = []
    for e in range(NEX):
        for wcache, wsrc, wssrc in [(w1c, ap["w1t"], ap["ws1t"]),
                                    (w3c, ap["w3t"], ap["ws3t"])]:
            src = (wsrc[e] if e < NE else wssrc).rearrange("(k p) i -> p k i", p=128)
            dstc = (wcache[e] if e < NE else wcache[8]).rearrange("(k p) i -> p k i", p=128)
            for k in range(DK):
                cast_jobs.append(("piece", src[:, k, :], dstc[:, k, :]))
        cast_jobs.append(("mark", ("w13", e)))
    w2r = w2c.rearrange("do (t p) c -> p do t c", p=128)
    for dq in range(4):
        for e in range(NEX):
            src = (ap["w2t"][e] if e < NE else ap["ws2t"]).rearrange("(j p) d -> p j d", p=128)
            for j in range(IC):
                cast_jobs.append(("piece", src[:, j, dq * 512:(dq + 1) * 512],
                                  w2r[:, dq * 4:(dq + 1) * 4, e * IC + j]))
        cast_jobs.append(("mark", ("w2", dq)))
    for wname in ("wq", "wk", "wv", "wo"):
        src = ap[wname].rearrange("(k p) d -> p k d", p=128)
        dstc = wqkvc[wname].rearrange("(k p) d -> p k d", p=128)
        for k in range(DK):
            for dh in range(2):
                cast_jobs.append(("piece", src[:, k, dh * 1024:(dh + 1) * 1024],
                                  dstc[:, k, dh * 1024:(dh + 1) * 1024]))
        cast_jobs.append(("mark", (wname,)))
    cast_jobs.reverse()

    def feed_casts(upto=None, n=None):
        if upto is not None and upto in cache_ready:
            return
        cnt = 0
        while cast_jobs:
            if n is not None and cnt >= n:
                return
            job = cast_jobs.pop()
            if job[0] == "mark":
                cache_ready[job[1]] = [(s, s.n) for s in s_cout]
                if upto is not None and job[1] == upto:
                    return
            else:
                cast_piece(job[1], job[2])
                cnt += 1

    PRE = {"dve": (s_dve, s_dve.n), "syq": (s_syq, s_syq.n), "gpq": (s_gpq, s_gpq.n)}
    ln_guard = {"stats": None, "pass2": None}

    def emit_ln(xin, xin_wait, ln_dst, want_gate, gate_tc, lnp, extra_p2_wait=None):
        xb_sb, tn_sb = lnp["xb"], lnp["tn"]
        xq_sb = xb_sb
        for dk in range(DK):
            sl_ = dk % 2
            xb = xb_sb[:, sl_ * TC:(sl_ + 1) * TC]
            wl = maxw(lnp["xb_free"][sl_], xin_wait if dk == 0 else None)
            p.op("scalar", "copy", out=xb, in_=xin[:, dk * TC:(dk + 1) * TC],
                 waits=wl, inc=s_act)
            cv = (s_act, s_act.n)
            wl = [cv]
            if dk == 0:
                wl = maxw(cv, bank_free[6])
            p.op("tensor", "matmul", B[6][:, :], c_ones_stats[:, :], xb,
                 start=(dk == 0), stop=(dk == DK - 1), waits=wl, inc=s_pe)
            lnp["xb_free"][sl_] = (s_pe, s_pe.n)
        for dk in range(DK):
            sl_ = dk % 2
            xq = xq_sb[:, sl_ * TC:(sl_ + 1) * TC]
            p.op("scalar", "activation", xq, xin[:, dk * TC:(dk + 1) * TC], AF.Square,
                 waits=maxw(lnp["xb_free"][sl_]), inc=s_act)
            qv = (s_act, s_act.n)
            wl = [qv]
            if dk == 0:
                wl = maxw(qv, bank_free[7])
            p.op("tensor", "matmul", B[7][:, :], c_ones_stats[:, :], xq,
                 start=(dk == 0), stop=(dk == DK - 1), waits=wl, inc=s_pe)
            lnp["xb_free"][sl_] = (s_pe, s_pe.n)
        rep_mm = (s_pe, s_pe.n)
        p.op("scalar", "copy", out=mean_rep[:, :], in_=B[6][:, :],
             waits=maxw(rep_mm, ln_guard["pass2"]), inc=s_act)
        mr = (s_act, s_act.n)
        p.op("vector", "tensor_tensor", msq_sb[:, :], mean_rep[:, :], mean_rep[:, :],
             op=OP.mult, waits=[mr], inc=s_dve)
        p.op("vector", "tensor_tensor", var_sb[:, :], B[7][:, :], msq_sb[:, :],
             op=OP.subtract, waits=[(s_dve, s_dve.n)], inc=s_dve)
        bank_free[7] = (s_dve, s_dve.n)
        p.op("scalar", "activation", lnv_sb[:, :], var_sb[:, :], AF.Ln,
             bias=c_eps[:, 0:1], waits=[(s_dve, s_dve.n)], inc=s_act)
        p.op("scalar", "activation", rstd_sb[:, :], lnv_sb[:, :], AF.Exp, scale=-0.5,
             waits=[(s_act, s_act.n)], inc=s_act)
        rstd_done = (s_act, s_act.n)
        bank_free[6] = mr
        for dk in range(DK):
            tn = tn_sb[:, :]
            wl = maxw(rstd_done, lnp["tn_free"][0], extra_p2_wait,
                      xin_wait if dk == 0 else None)
            p.op("vector", "tensor_tensor", tn, xin[:, dk * TC:(dk + 1) * TC],
                 mean_rep[:, :], op=OP.subtract, waits=wl, inc=s_dve)
            p.op("vector", "tensor_tensor", ln_dst[:, dk * TC:(dk + 1) * TC],
                 tn, rstd_sb[:, :], op=OP.mult, waits=[(s_dve, s_dve.n)], inc=s_dve)
            lnp["tn_free"][0] = (s_dve, s_dve.n)
        ln_guard["pass2"] = (s_dve, s_dve.n)
        ln_fill = (s_dve, s_dve.n)
        if not want_gate:
            return ln_fill
        g = gate_t
        for tt in range(4):
            for dk in range(DK):
                wl = []
                if dk == 0:
                    wl = maxw(ln_fill, bank_free[6])
                p.op("tensor", "matmul", B[6][0:128, 0:8],
                     ln_dst[:, dk * TC + tt * 128: dk * TC + tt * 128 + 128],
                     gwt_sb[:, dk * 8:(dk + 1) * 8],
                     start=(dk == 0), stop=(dk == DK - 1), waits=wl,
                     inc=(s_pe if dk == DK - 1 else None))
            gmm = (s_pe, s_pe.n)
            p.op("vector", "tensor_copy", g["lg"][:, :], B[6][0:128, 0:8],
                 waits=[gmm], inc=s_dve)
            bank_free[6] = (s_dve, s_dve.n)
            p.op("vector", "reduce_max", g["m1"][:, :], g["lg"][:, :], axis=AX.X,
                 waits=[(s_dve, s_dve.n)], inc=s_dve)
            p.op("vector", "tensor_scalar", g["d1"][:, :], g["lg"][:, :],
                 g["m1"][:, 0:1], None, op0=OP.subtract,
                 waits=[(s_dve, s_dve.n)], inc=s_dve)
            p.op("scalar", "activation", g["sc"][:, :], g["d1"][:, :], AF.Exp,
                 waits=[(s_dve, s_dve.n)], inc=s_act)
            p.op("vector", "reduce_sum", g["m1"][:, :], g["sc"][:, :], axis=AX.X,
                 waits=[(s_act, s_act.n)], inc=s_dve)
            p.op("vector", "reciprocal", g["rc"][:, 0:1], g["m1"][:, 0:1],
                 waits=[(s_dve, s_dve.n)], inc=s_dve)
            for f, a, kw in [
                ("tensor_scalar", (g["sc"][:, :], g["sc"][:, :], g["rc"][:, 0:1], None), dict(op0=OP.mult)),
                ("tensor_tensor", (g["bi"][:, :], g["sc"][:, :], gbrep[:, :]), dict(op=OP.add)),
                ("reduce_max", (g["m1"][:, :], g["bi"][:, :]), dict(axis=AX.X)),
                ("tensor_scalar", (g["d1"][:, :], g["bi"][:, :], g["m1"][:, 0:1], None), dict(op0=OP.subtract)),
                ("tensor_single_scalar", (g["mk1"][:, :], g["d1"][:, :], 0.0), dict(op=OP.is_ge)),
                ("scalar_tensor_tensor", (g["b2"][:, :], g["mk1"][:, :], -1e30, g["bi"][:, :]), dict(op0=OP.mult, op1=OP.add)),
                ("reduce_max", (g["m2"][:, :], g["b2"][:, :]), dict(axis=AX.X)),
                ("tensor_scalar", (g["d1"][:, :], g["b2"][:, :], g["m2"][:, 0:1], None), dict(op0=OP.subtract)),
                ("tensor_single_scalar", (g["mk2"][:, :], g["d1"][:, :], 0.0), dict(op=OP.is_ge)),
                ("tensor_tensor", (g["mk1"][:, :], g["mk1"][:, :], g["mk2"][:, :]), dict(op=OP.add)),
                ("scalar_tensor_tensor", (g["dwf"][:, :], g["sc"][:, :], 0.5, g["mk1"][:, :]), dict(op0=OP.mult, op1=OP.mult)),
            ]:
                p.op("vector", f, *a, waits=[(s_dve, s_dve.n)], inc=s_dve, **kw)
            p.op("tensor", "matmul", B[7][0:8, 0:128], g["dwf"][:, :], c_ident[:, :],
                 start=True, stop=True,
                 waits=maxw((s_dve, s_dve.n), bank_free[7]), inc=s_pe)
            p.op("vector", "tensor_copy",
                 dwT[0:8, gate_tc * TC + tt * 128: gate_tc * TC + tt * 128 + 128],
                 B[7][0:8, 0:128], waits=[(s_pe, s_pe.n)], inc=s_dve)
            bank_free[7] = (s_dve, s_dve.n)
        return ln_fill

    def mk_lnp(ssb, tag):
        return {
            "xb": ssb("xb" + tag, [128, 2 * TC], BF16),
            "tn": ssb("tn" + tag, [128, TC], F32),
            "xb_free": [None] * 2, "tn_free": [None],
        }

    cache_state = {}

    def emit_ffn(inv, x_in, x_out, fence, x_in_store):
        sfx = f"_f{inv}"
        scope = []
        def ssb(name, shape, dt):
            t = nc.sbuf_tensor(name + sfx, shape, dt)
            h = t.__enter__()
            scope.append(t)
            return h

        ln_sb = ssb("ln", [128, 2 * DK * TC], BF16)
        hs_sb = ssb("hs", [128, NEX * IC * TC], BF16)
        w13_sb = ssb("w13", [128, 3 * DK * 256], BF16)
        w2_sb = ssb("w2", [128, 2 * 36 * 128], BF16)
        silu_sb = ssb("silu", [128, TC], BF16)
        hmul_sb = ssb("hmul", [128, TC], BF16)
        xin_sb = ssb("xin", [128, DK * TC], F32)
        xo_sb = ssb("xo", [128, TC], F32)
        lnp = mk_lnp(ssb, sfx)
        fl = fence if isinstance(fence, list) else [fence]
        xin_free = fl
        w13_free = [fl] * 3
        w2_free = [fl] * 2
        xo_free = [fl]
        ln_free = [fl, fl]
        hs_free = fl
        silu_free = [fl]
        use = {"w13": 0, "w2": 0, "xo": 0}
        for k in ("xb_free", "tn_free"):
            lnp[k] = [fl] * len(lnp[k])

        out_sem = s_out if (x_out is out_x3) else s_scq
        out_store_val = [None] * NTC

        for tc in range(NTC):
            lslot = tc % 2
            lnd = ln_sb[:, lslot * DK * TC:(lslot + 1) * DK * TC]
            wl = maxw(xin_free)
            if x_in_store is not None:
                wl = maxw(*wl, x_in_store[tc])
            p.op("sync", "dma_start", waits=wl, inc=s_syq, inc_k=16,
                 out=xin_sb[:, :].rearrange("p (k t) -> p k t", k=DK),
                 in_=x_in.rearrange("(k p) t -> p k t", p=128)[:, :, tc * TC:(tc + 1) * TC])
            xin_fill = (s_syq, s_syq.n)
            if inv == 0:
                lf = emit_ln(xin_sb[:, :], xin_fill, lnd, debug_stop != "ln1", tc,
                             lnp, extra_p2_wait=ln_free[lslot])
                dw_ready = (s_dve, s_dve.n)
                if debug_stop == "stats" and tc == 0:
                    p.op("gpsimd", "dma_start", waits=maxw(lf), inc=s_out, inc_k=16,
                         out=out_x3.rearrange("(k p) t -> p k t", p=128)[:, 0, 0:TC],
                         in_=mean_rep[:, :])
                    p.op("gpsimd", "dma_start", waits=maxw(lf), inc=s_out, inc_k=16,
                         out=out_x3.rearrange("(k p) t -> p k t", p=128)[:, 1, 0:TC],
                         in_=rstd_sb[:, :])
                    p.op("gpsimd", "dma_start", waits=maxw(lf), inc=s_out, inc_k=16,
                         out=out_x3.rearrange("(k p) t -> p k t", p=128)[:, 2, 0:TC],
                         in_=lnv_sb[:, :])
                    p.op("gpsimd", "dma_start", waits=maxw(lf), inc=s_out, inc_k=16,
                         out=out_x3.rearrange("(k p) t -> p k t", p=128)[:, 3, 0:TC],
                         in_=lnd[:, 0:TC])
                if debug_stop == "ln1":
                    lnrel = p.drain("vector", s_dve)
                    if False:
                        for j, src in enumerate([mean_rep[:, :], rstd_sb[:, :],
                                                 msq_sb[:, :]]):
                            p.op("gpsimd", "dma_start", waits=maxw(lf, lnrel),
                                 inc=s_out, inc_k=16,
                                 out=out_x3.rearrange("(k p) t -> p k t", p=128)[:, j, tc * TC:(tc + 1) * TC],
                                 in_=src)
                        p.op("gpsimd", "dma_start", waits=maxw(lf, lnrel),
                             inc=s_out, inc_k=16,
                             out=out_x3.rearrange("(k p) t -> p k t", p=128)[:, 3, tc * TC:tc * TC + 128],
                             in_=c_ones_stats[:, :])
                    else:
                        p.op("sync", "dma_start", waits=maxw(lf, lnrel), inc=s_syq, inc_k=16,
                             out=lnbd.rearrange("(k p) t -> p k t", p=128)[:, :, tc * TC:(tc + 1) * TC],
                             in_=lnd.rearrange("p (k t) -> p k t", k=DK))
                    ln_free[lslot] = [(s_syq, s_syq.n)]
                    if tc == NTC - 1:
                        p.op("gpsimd", "dma_start", waits=[(s_syq, s_syq.n)],
                             inc=s_out, inc_k=16, out=out_x3, in_=lnbd)
            else:
                wl = maxw(ln_free[lslot], fence)
                p.op("sync", "dma_start", waits=wl, inc=s_syq, inc_k=16,
                     out=lnd.rearrange("p (k t) -> p k t", k=DK),
                     in_=ln3d.rearrange("(c k p) t -> p c k t", c=NTC, p=128)[:, tc])
                lf = (s_syq, s_syq.n)
                dw_ready = None
            ln_ready = lf
            if debug_stop in ("ln1", "stats"):
                xin_free = [p.drain("vector", s_dve)]
                continue

            dwrep_ready = {}
            for e in range(NEX):
                if inv == 0:
                    if tc == 0:
                        feed_casts(upto=("w13", min(e + 1, NEX - 1)))
                        if e == NEX - 1:
                            feed_casts(upto=("w2", 0))
                    else:
                        feed_casts(n=6)
                if e < NE:
                    dbank = 4 + (e % 2)
                    p.op("tensor", "matmul", B[dbank][:, :],
                         c_oh8[0:8, e * 128:(e + 1) * 128], dwT[0:8, tc * TC:(tc + 1) * TC],
                         start=True, stop=True,
                         waits=maxw(dw_ready, bank_free[dbank], PRE["syq"],
                                    fence if inv == 1 else None), inc=s_pe)
                    dwrep_ready[e] = (s_pe, s_pe.n)
                for iq in range(4):
                    wslots = []
                    for wi, wcache in enumerate([w1c, w3c]):
                        s = use["w13"] % 3
                        use["w13"] += 1
                        dst = w13_sb[:, s * DK * 256:(s + 1) * DK * 256]
                        dst3 = dst.rearrange("p (k i) -> p k i", k=DK)
                        srcc = (wcache[e] if e < NE else wcache[8]).rearrange(
                            "(k p) i -> p k i", p=128)[:, :, iq * 256:(iq + 1) * 256]
                        wl = maxw(w13_free[s], cache_ready[("w13", e)])
                        p.op("sync", "dma_start", waits=wl, inc=s_syq, inc_k=16,
                             out=dst3, in_=srcc)
                        fillv = (s_syq, s_syq.n)
                        wslots.append((s, dst, fillv, None))
                    for ici in range(2):
                        ic = iq * 2 + ici
                        pa, pb = (0, 1) if (ic % 2 == 0) else (2, 3)
                        for (bank, (s, dst, fillv, wb)) in zip((pa, pb), wslots):
                            for dk in range(DK):
                                wl = []
                                if dk == 0:
                                    wl = maxw(fillv, bank_free[bank], ln_ready)
                                p.op("tensor", "matmul", B[bank][:, :],
                                     dst[:, dk * 256 + ici * 128: dk * 256 + ici * 128 + 128],
                                     lnd[:, dk * TC:(dk + 1) * TC],
                                     start=(dk == 0), stop=(dk == DK - 1), waits=wl,
                                     inc=(s_pe if dk == DK - 1 else None))
                            if ici == 1:
                                w13_free[s] = [(s_pe, s_pe.n), wb] if wb else [(s_pe, s_pe.n)]
                        mmv = (s_pe, s_pe.n)
                        sil = silu_sb[:, :]
                        hm = hmul_sb[:, :]
                        p.op("scalar", "activation", sil, B[pa][:, :], AF.Silu,
                             waits=maxw(mmv, silu_free[0]), inc=s_act)
                        bank_free[pa] = (s_act, s_act.n)
                        p.op("vector", "tensor_tensor", hm, sil, B[pb][:, :], op=OP.mult,
                             waits=[(s_act, s_act.n)], inc=s_dve)
                        bank_free[pb] = (s_dve, s_dve.n)
                        silu_free[0] = (s_dve, s_dve.n)
                        hdst = hs_sb[:, (e * IC + ic) * TC:(e * IC + ic + 1) * TC]
                        if e < NE:
                            dbank = 4 + (e % 2)
                            p.op("vector", "tensor_tensor", hdst, hm, B[dbank][:, :],
                                 op=OP.mult,
                                 waits=maxw(dwrep_ready[e], hs_free), inc=s_dve)
                            if ic == IC - 1:
                                bank_free[dbank] = (s_dve, s_dve.n)
                        else:
                            p.op("vector", "tensor_scalar", hdst, hm, 0.5, None,
                                 op0=OP.mult, waits=maxw(hs_free), inc=s_dve)
            hs_done = (s_dve, s_dve.n)
            ln_free[lslot] = [(s_pe, s_pe.n)]

            for dout in range(DK):
                if inv == 0 and tc == 0:
                    feed_casts(upto=("w2", min(dout // 4 + 1, 3)))
                hA, hB = 0, 1
                dstA = w2_sb[:, hA * 36 * 128:(hA + 1) * 36 * 128]
                dstB = w2_sb[:, hB * 36 * 128:(hB + 1) * 36 * 128]
                w2cr = w2c[dout].rearrange("(j p) c -> p j c", p=128)
                p.op("sync", "dma_start",
                     waits=maxw(w2_free[hA], cache_ready[("w2", dout // 4)]),
                     inc=s_syq, inc_k=16,
                     out=dstA.rearrange("p (j c) -> p j c", c=128), in_=w2cr[:, 0:36])
                fillA = (s_syq, s_syq.n)
                p.op("sync", "dma_start", waits=maxw(w2_free[hB]),
                     inc=s_syq, inc_k=16,
                     out=dstB.rearrange("p (j c) -> p j c", c=128), in_=w2cr[:, 36:72])
                fillB = (s_syq, s_syq.n)
                xr = xin_sb[:, dout * TC:(dout + 1) * TC]
                xr_fill = xin_fill
                bank = 2 + (dout % 2)
                for nmm in range(NEX * IC):
                    wl = []
                    if nmm == 0:
                        wl = maxw(fillA, hs_done, bank_free[bank])
                    elif nmm == 36:
                        wl = [fillB]
                    src_t = (dstA[:, nmm * 128:(nmm + 1) * 128] if nmm < 36
                             else dstB[:, (nmm - 36) * 128:(nmm - 35) * 128])
                    p.op("tensor", "matmul", B[bank][:, :], src_t,
                         hs_sb[:, nmm * TC:(nmm + 1) * TC],
                         start=(nmm == 0), stop=(nmm == NEX * IC - 1), waits=wl,
                         inc=(s_pe if nmm in (35, NEX * IC - 1) else None))
                    if nmm == 35:
                        w2_free[hA] = [(s_pe, s_pe.n)]
                ymm = (s_pe, s_pe.n)
                w2_free[hB] = [ymm]
                os_ = 0
                xo = xo_sb[:, :]
                p.op("vector", "tensor_tensor", xo, B[bank][:, :], xr, op=OP.add,
                     waits=maxw(ymm, xr_fill, xo_free[os_]), inc=s_dve)
                bank_free[bank] = (s_dve, s_dve.n)
                xod = p.drain("vector", s_dve)
                p.op("scalar", "dma_start", waits=[xod],
                     inc=out_sem, inc_k=16,
                     out=x_out.rearrange("(k p) t -> p k t", p=128)[:, dout, tc * TC:(tc + 1) * TC],
                     in_=xo)
                xo_free[os_] = [(out_sem, out_sem.n)]
            hs_free = [(s_pe, s_pe.n)]
            xin_free = [(s_dve, s_dve.n)]
            out_store_val[tc] = (out_sem, out_sem.n)
        if inv == 0:
            feed_casts()
        for t in reversed(scope):
            t.__exit__(None, None, None)
        end = maxw((s_pe, s_pe.n), (s_dve, s_dve.n), (s_act, s_act.n),
                   (s_syq, s_syq.n), (s_scq, s_scq.n), (out_sem, out_sem.n))
        return out_store_val, end

    def emit_attn(x_in, x_out, fence, x1_store):
        scope = []
        def ssb(name, shape, dt):
            t = nc.sbuf_tensor(name + "_at", shape, dt)
            h = t.__enter__()
            scope.append(t)
            return h

        wo_sb = ssb("wo", [128, DK * D], BF16)
        lnp = mk_lnp(ssb, "_at")
        scope1 = []
        def ssb1(name, shape, dt):
            t = nc.sbuf_tensor(name + "_at", shape, dt)
            h = t.__enter__()
            scope1.append(t)
            return h
        wbuf_sb = ssb1("wbuf", [128, 2 * DK * TC], BF16)
        ln2_sb = ssb1("ln2", [128, DK * TC], BF16)
        stage_sb = ssb1("stage", [128, 16 * TC], BF16)
        xin_sb = ssb1("xina", [128, DK * TC], F32)
        bvrep = ssb1("bvrep", [128, D], F32)

        fl = fence if isinstance(fence, list) else [fence]
        for k in ("xb_free", "tn_free"):
            lnp[k] = [fl] * len(lnp[k])
        xin_free = fl
        w_free = [fl] * 2
        use = {"w": 0, "q": 0, "k": 0, "v": 0, "p": 0, "ot": 0, "xra": 0}
        out_sem = s_out if (x_out is out_x3) else s_scq

        p.op("sync", "dma_start", waits=maxw(cache_ready[("wo",)], *fl),
             inc=s_syq, inc_k=16,
             out=wo_sb[:, :].rearrange("p (k d) -> p k d", k=DK),
             in_=wqkvc["wo"].rearrange("(k p) d -> p k d", p=128))
        wo_ready = (s_syq, s_syq.n)
        p.op("sync", "dma_start", waits=maxw(fl), inc=s_syq, inc_k=16,
             out=bvrep[:, :], in_=ap["bv_row"])
        bvrep_ready = (s_syq, s_syq.n)

        ag_done = {}
        ln2_free = None
        qkv_cache_wb = None
        stage_guard = None
        for tc in range(NTC):
            p.op("sync", "dma_start",
                 waits=maxw(xin_free, x1_store[tc], *fl), inc=s_syq, inc_k=16,
                 out=xin_sb[:, :].rearrange("p (k t) -> p k t", k=DK),
                 in_=x_in.rearrange("(k p) t -> p k t", p=128)[:, :, tc * TC:(tc + 1) * TC])
            xin_fill = (s_syq, s_syq.n)
            lf = emit_ln(xin_sb[:, :], xin_fill, ln2_sb[:, :], False, tc, lnp,
                         extra_p2_wait=ln2_free)
            xin_free = [p.drain("vector", s_dve)]
            ln2_ready = lf
            kst = None
            for wi, wname in enumerate(("wq", "wk")):
                for oc in range(4):
                    ws = use["w"] % 2
                    use["w"] += 1
                    wdst = wbuf_sb[:, ws * DK * TC:(ws + 1) * DK * TC]
                    wdst3 = wdst.rearrange("p (k c) -> p k c", k=DK)
                    p.op("sync", "dma_start",
                         waits=maxw(w_free[ws], cache_ready[(wname,)], *fl),
                         inc=s_syq, inc_k=16, out=wdst3,
                         in_=wqkvc[wname].rearrange("(k p) c -> p k c", p=128)[:, :, oc * TC:(oc + 1) * TC])
                    wfill = (s_syq, s_syq.n)
                    wwb = None
                    for hc in range(4):
                        bank = hc % 2
                        for dk in range(DK):
                            wl = []
                            if dk == 0:
                                wl = maxw(wfill, bank_free[bank], ln2_ready)
                            p.op("tensor", "matmul", B[bank][:, :],
                                 wdst[:, dk * TC + hc * 128: dk * TC + hc * 128 + 128],
                                 ln2_sb[:, dk * TC:(dk + 1) * TC],
                                 start=(dk == 0), stop=(dk == DK - 1), waits=wl,
                                 inc=(s_pe if dk == DK - 1 else None))
                        mmv = (s_pe, s_pe.n)
                        if hc == 3:
                            w_free[ws] = [mmv, wwb] if wwb else [mmv]
                        hglob = oc * 4 + hc
                        p.op("vector", "tensor_scalar",
                             stage_sb[:, hglob * TC:(hglob + 1) * TC], B[bank][:, :],
                             bias_sb["bq" if wi == 0 else "bk"][:, hglob:hglob + 1], None,
                             op0=OP.add, waits=maxw(mmv, stage_guard), inc=s_dve)
                        bank_free[bank] = (s_dve, s_dve.n)
                stg = p.drain("vector", s_dve)
                dest = qTd if wi == 0 else kcd
                p.op("scalar", "dma_start", waits=[stg], inc=s_scq, inc_k=16,
                     out=dest.rearrange("(c h p) t -> p c h t", c=NTC, p=128)[:, tc],
                     in_=stage_sb[:, :].rearrange("p (h t) -> p h t", h=16))
                stage_guard = (s_scq, s_scq.n)
                if wi == 1:
                    kst = stage_guard
            for oc in range(4):
                ws = use["w"] % 2
                use["w"] += 1
                wdst = wbuf_sb[:, ws * DK * TC:(ws + 1) * DK * TC]
                wdst3 = wdst.rearrange("p (k c) -> p k c", k=DK)
                p.op("sync", "dma_start",
                     waits=maxw(w_free[ws], cache_ready[("wv",)], *fl),
                     inc=s_syq, inc_k=16, out=wdst3,
                     in_=wqkvc["wv"].rearrange("(k p) c -> p k c", p=128)[:, :, oc * TC:(oc + 1) * TC])
                wfill = (s_syq, s_syq.n)
                wwb = None
                for tt in range(4):
                    bank = tt % 2
                    for dk in range(DK):
                        wl = []
                        if dk == 0:
                            wl = maxw(wfill, bank_free[bank], ln2_ready)
                        p.op("tensor", "matmul", B[bank][:, :],
                             ln2_sb[:, dk * TC + tt * 128: dk * TC + tt * 128 + 128],
                             wdst[:, dk * TC:(dk + 1) * TC],
                             start=(dk == 0), stop=(dk == DK - 1), waits=wl,
                             inc=(s_pe if dk == DK - 1 else None))
                    mmv = (s_pe, s_pe.n)
                    if tt == 3:
                        w_free[ws] = [mmv, wwb] if wwb else [mmv]
                    p.op("vector", "tensor_tensor",
                         stage_sb[:, (tt * 4 + oc) * TC:(tt * 4 + oc + 1) * TC],
                         B[bank][:, :], bvrep[:, oc * TC:(oc + 1) * TC], op=OP.add,
                         waits=maxw(mmv, stage_guard, bvrep_ready), inc=s_dve)
                    bank_free[bank] = (s_dve, s_dve.n)
            stg = p.drain("vector", s_dve)
            if debug_stop == "den" and tc == 0:
                p.op("vector", "tensor_copy", xin_sb[:, 0:TC], stage_sb[:, 0:TC],
                     waits=[stg], inc=s_dve)
                dnr4 = p.drain("vector", s_dve)
                p.op("scalar", "dma_start", waits=[dnr4], inc=s_out, inc_k=16,
                     out=out_x3.rearrange("(k p) t -> p k t", p=128)[:, 14, TC:2 * TC],
                     in_=xin_sb[:, 0:TC])
            p.op("scalar", "dma_start", waits=[stg], inc=s_scq, inc_k=16,
                 out=vcd[tc * TC:(tc + 1) * TC, :].rearrange("(tt p) c -> p tt c", p=128),
                 in_=stage_sb[:, :].rearrange("p (tt c) -> p tt c", tt=4))
            vst = (s_scq, s_scq.n)
            stage_guard = vst
            ln2_free = (s_pe, s_pe.n)
            p.wait("gpsimd", kst)
            p.op("gpsimd", "collective_compute", "AllGather", OP.bypass,
                 replica_groups=[[0, 1], [2, 3], [4, 5], [6, 7]],
                 ins=[kcd[tc * D:(tc + 1) * D, :].opt()],
                 outs=[gkd[tc * 2 * D:(tc + 1) * 2 * D, :].opt()],
                 inc=s_cc)
            p.wait("gpsimd", vst)
            p.op("gpsimd", "collective_compute", "AllGather", OP.bypass,
                 replica_groups=[[0, 1], [2, 3], [4, 5], [6, 7]],
                 ins=[vcd[tc * TC:(tc + 1) * TC, :].opt()],
                 outs=[gvd[tc * 2 * TC:(tc + 1) * 2 * TC, :].opt()],
                 inc=s_cc)
            ag_done[tc] = (s_cc, s_cc.n)
        qT_store = stage_guard
        for t in reversed(scope1):
            t.__exit__(None, None, None)
        qkv_end = maxw((s_pe, s_pe.n), (s_dve, s_dve.n), (s_act, s_act.n),
                       (s_syq, s_syq.n), (s_gpq, s_gpq.n))
        fl2 = qkv_end
        if debug_stop == "vdump":
            dmb = ssb("dmb", [128, D], BF16)
            dmf = ssb("dmf", [128, D], F32)
            Q = maxw(*fl2, ag_done[NTC - 1], (s_scq, s_scq.n))
            ldfree = None
            cpfree = None
            for blk in range(16):
                p.op("sync", "dma_start", waits=maxw(*Q, ldfree),
                     inc=s_syq, inc_k=16, out=dmb[:, :],
                     in_=gvd[blk * 128:(blk + 1) * 128, :])
                p.op("vector", "tensor_copy", dmf[:, :], dmb[:, :],
                     waits=maxw((s_syq, s_syq.n), cpfree), inc=s_dve)
                ldfree = p.drain("vector", s_dve)
                p.op("scalar", "dma_start", waits=[ldfree], inc=s_out, inc_k=16,
                     out=out_x3.rearrange("(k p) t -> p k t", p=128)[:, blk],
                     in_=dmf[:, :])
                cpfree = (s_out, s_out.n)
            for t in reversed(scope):
                t.__exit__(None, None, None)
            return [None] * NTC, maxw((s_out, s_out.n))
        msk_sb = ssb("msk", [128, NMASK * TC], BF16)
        q_sb = ssb("qs", [128, 2 * TC], BF16)
        kbuf_sb = ssb("kbuf", [128, 3 * TC], BF16)
        vbuf_sb = ssb("vbuf", [128, 3 * TC], BF16)
        pt_sb = ssb("pt", [128, 5 * TC], BF16)
        lnden_sb = ssb("lnden", [128, 2 * TC], F32)
        attnN_sb = ssb("attnN", [128, NH * TC], BF16)
        x2st_sb = ssb("x2st", [128, DK * TC], F32)
        ot_sb = ssb("ot", [128, TC], F32)
        xra_sb = ssb("xra", [128, 2 * TC], F32)
        ln3_sb = ssb("ln3", [128, DK * TC], BF16)
        q_free = [fl2] * 2
        k_free = [fl2] * 3
        v_free = [fl2] * 3
        p_free = [fl2] * 5
        ot_free = [fl2] * 2
        xra_free = [fl2] * 2

        x2_store = [None] * NTC
        msk_free = None
        lnden_guard = [None, None]
        attnN_guard = None
        x2st_guard = [None] * DK
        x2st_fill = [None] * DK
        ln3_guard = None
        for sl in range(4):
            p.op("sync", "dma_start", waits=maxw(msk_free, *fl2), inc=s_syq, inc_k=16,
                 out=msk_sb[:, :].rearrange("p (m t) -> p m t", m=NMASK),
                 in_=ap["maskseq"].rearrange("(s m p) t -> p s m t", s=4, p=128)[:, sl])
            msk_fill = (s_syq, s_syq.n)
            cnt = CNT[sl]
            ag_need = ag_done[1] if cnt <= 16 else ag_done[3]
            for h in range(NH):
                qs = use["q"] % 2
                use["q"] += 1
                qt = q_sb[:, qs * TC:(qs + 1) * TC]
                p.op("sync", "dma_start", waits=maxw(q_free[qs], qT_store),
                     inc=s_syq, inc_k=16, out=qt,
                     in_=qTd.rearrange("(c hh p) t -> p c hh t", c=NTC, p=128)[:, sl, h])
                q_fill = (s_syq, s_syq.n)
                ngrp = cnt // 4
                kg_fill = [None] * ngrp
                vg_fill = [None] * ngrp
                kg_slot = [0] * ngrp
                vg_slot = [0] * ngrp

                def load_group(gidx):
                    r, c, _ = _kbmap(gidx * 4)
                    ks = use["k"] % 3
                    use["k"] += 1
                    kg_slot[gidx] = ks
                    p.op("sync", "dma_start", waits=maxw(k_free[ks], ag_need),
                         inc=s_syq, inc_k=16,
                         out=kbuf_sb[:, ks * TC:(ks + 1) * TC],
                         in_=gkd[c * 2 * D + r * D + h * 128: c * 2 * D + r * D + (h + 1) * 128, :])
                    kg_fill[gidx] = (s_syq, s_syq.n)
                    vs = use["v"] % 3
                    use["v"] += 1
                    vg_slot[gidx] = vs
                    p.op("sync", "dma_start", waits=maxw(v_free[vs], ag_need),
                         inc=s_syq, inc_k=16,
                         out=vbuf_sb[:, vs * TC:(vs + 1) * TC].rearrange("p (tt c) -> p tt c", tt=4),
                         in_=gvd[c * 2 * TC + r * TC:c * 2 * TC + (r + 1) * TC,
                                 h * 128:(h + 1) * 128].rearrange("(tt p) c -> p tt c", p=128))
                    vg_fill[gidx] = (s_syq, s_syq.n)

                for gidx in range(min(3, ngrp)):
                    load_group(gidx)
                abank = 2 + (h % 2)
                dbank = 4 + (h % 2)
                SHIFT = 3
                ptm = [None] * cnt
                last_score = None
                for step in range(cnt + SHIFT):
                    kb = step
                    if kb < cnt:
                        g_ = kb // 4
                        o = kb % 4
                        sbank = kb % 2
                        wl = maxw(kg_fill[g_], q_fill if kb == 0 else None,
                                  bank_free[sbank])
                        p.op("tensor", "matmul", B[sbank][:, :],
                             kbuf_sb[:, kg_slot[g_] * TC + o * 128: kg_slot[g_] * TC + o * 128 + 128],
                             qt, start=True, stop=True, waits=wl, inc=s_pe)
                        smm = (s_pe, s_pe.n)
                        if kb == cnt - 1:
                            last_score = smm
                        ps = use["p"] % 5
                        use["p"] += 1
                        pt = pt_sb[:, ps * TC:(ps + 1) * TC]
                        p.op("scalar", "activation", pt, B[sbank][:, :], AF.Exp,
                             scale=SCALE, waits=maxw(smm, p_free[ps]), inc=s_act)
                        bank_free[sbank] = (s_act, s_act.n)
                        pv = (s_act, s_act.n)
                        if MASKSTART[sl] <= kb < MASKSTART[sl] + NMASK:
                            mi = kb - MASKSTART[sl]
                            p.op("vector", "tensor_tensor", pt, pt,
                                 msk_sb[:, mi * TC:(mi + 1) * TC], op=OP.mult,
                                 waits=maxw(pv, msk_fill), inc=s_dve)
                            pv = (s_dve, s_dve.n)
                        ptm[kb] = (ps, pt, pv)
                    kc = step - SHIFT
                    if kc >= 0:
                        g2 = kc // 4
                        o2 = kc % 4
                        ps2, pt2, pv2 = ptm[kc]
                        wl = maxw(pv2, vg_fill[g2],
                                  bank_free[abank] if kc == 0 else None)
                        p.op("tensor", "matmul", B[abank][:, :],
                             vbuf_sb[:, vg_slot[g2] * TC + o2 * 128: vg_slot[g2] * TC + o2 * 128 + 128],
                             pt2, start=(kc == 0), stop=(kc == cnt - 1), waits=wl,
                             inc=(s_pe if kc == cnt - 1 else None))
                        wl = maxw(bank_free[dbank] if kc == 0 else None,
                                  PRE["syq"] if kc == 0 else None)
                        p.op("tensor", "matmul", B[dbank][:, :], c_ones128[:, :],
                             pt2, start=(kc == 0), stop=(kc == cnt - 1), waits=wl,
                             inc=s_pe)
                        pcons = (s_pe, s_pe.n)
                        p_free[ps2] = [pcons]
                        if o2 == 3 or kc == cnt - 1:
                            k_free[kg_slot[g2]] = [pcons]
                            v_free[vg_slot[g2]] = [pcons]
                            if g2 + 3 < ngrp:
                                load_group(g2 + 3)
                q_free[qs] = [last_score]
                den_mm = (s_pe, s_pe.n)
                ls = h % 2
                p.op("scalar", "activation", lnden_sb[:, ls * TC:(ls + 1) * TC],
                     B[dbank][:, :], AF.Ln, waits=maxw(den_mm, lnden_guard[ls]), inc=s_act)
                bank_free[dbank] = (s_act, s_act.n)
                p.op("scalar", "activation", lnden_sb[:, ls * TC:(ls + 1) * TC],
                     lnden_sb[:, ls * TC:(ls + 1) * TC], AF.Exp, scale=-1.0,
                     waits=[(s_act, s_act.n)], inc=s_act)
                rv = (s_act, s_act.n)
                p.op("vector", "tensor_tensor", attnN_sb[:, h * TC:(h + 1) * TC],
                     B[abank][:, :], lnden_sb[:, ls * TC:(ls + 1) * TC], op=OP.mult,
                     waits=maxw(rv, attnN_guard), inc=s_dve)
                if debug_stop == "den" and sl == 1 and h < 4:
                    p.op("vector", "tensor_copy", ot_sb[:, 0:TC], qt,
                         waits=[(s_dve, s_dve.n)], inc=s_dve)
                    dnr = p.drain("vector", s_dve)
                    p.op("scalar", "dma_start", waits=[dnr], inc=s_out, inc_k=16,
                         out=out_x3.rearrange("(k p) t -> p k t", p=128)[:, 8 + h, 0:TC],
                         in_=ot_sb[:, 0:TC])
                    ot_free[0] = [(s_out, s_out.n)]
                bank_free[abank] = (s_dve, s_dve.n)
                lnden_guard[ls] = (s_dve, s_dve.n)
            attn_done = (s_dve, s_dve.n)
            msk_free = p.drain("vector", s_dve)
            if debug_stop == "ndump" and sl == 1:
                QD = maxw(attn_done, msk_free, (s_pe, s_pe.n), (s_act, s_act.n),
                          (s_syq, s_syq.n), (s_scq, s_scq.n), (s_cc, s_cc.n),
                          (out_sem, out_sem.n), (s_out, s_out.n))
                p.op("vector", "tensor_copy", x2st_sb[:, :], attnN_sb[:, :],
                     waits=QD, inc=s_dve)
                dnr = p.drain("vector", s_dve)
                p.op("scalar", "dma_start", waits=[dnr], inc=s_out, inc_k=16,
                     out=out_x3.rearrange("(k p) t -> p k t", p=128)[:, :, 0:TC],
                     in_=x2st_sb[:, :].rearrange("p (k t) -> p k t", k=DK))
                for t in reversed(scope):
                    t.__exit__(None, None, None)
                return [None] * NTC, maxw((s_out, s_out.n))
            if debug_stop == "den" and sl == 1:
                for hh in range(4):
                    p.op("vector", "tensor_copy", ot_sb[:, 0:TC],
                         attnN_sb[:, hh * TC:(hh + 1) * TC],
                         waits=maxw(attn_done, ot_free[0]), inc=s_dve)
                    dnr2 = p.drain("vector", s_dve)
                    p.op("scalar", "dma_start", waits=[dnr2], inc=s_out, inc_k=16,
                         out=out_x3.rearrange("(k p) t -> p k t", p=128)[:, hh, 0:TC],
                         in_=ot_sb[:, 0:TC])
                    ot_free[0] = [(s_out, s_out.n)]
            for dout in range(DK):
                bank = 6 + (dout % 2)
                rs = use["xra"] % 2
                use["xra"] += 1
                xr = xra_sb[:, rs * TC:(rs + 1) * TC]
                p.op("sync", "dma_start", waits=maxw(xra_free[rs], x1_store[sl]),
                     inc=s_syq, inc_k=16, out=xr,
                     in_=x_in.rearrange("(k p) t -> p k t", p=128)[:, dout, sl * TC:(sl + 1) * TC])
                xr_fill = (s_syq, s_syq.n)
                for dk in range(DK):
                    wl = []
                    if dk == 0:
                        wl = maxw(attn_done, wo_ready, bank_free[bank])
                    p.op("tensor", "matmul", B[bank][:, :],
                         wo_sb[:, dk * D + dout * 128: dk * D + dout * 128 + 128],
                         attnN_sb[:, dk * TC:(dk + 1) * TC],
                         start=(dk == 0), stop=(dk == DK - 1), waits=wl,
                         inc=(s_pe if dk == DK - 1 else None))
                omm = (s_pe, s_pe.n)
                ot = ot_sb[:, 0:TC]
                p.op("vector", "tensor_scalar", ot, B[bank][:, :],
                     bias_sb["bo"][:, dout:dout + 1], None, op0=OP.add,
                     waits=maxw(omm, ot_free[0]), inc=s_dve)
                bank_free[bank] = (s_dve, s_dve.n)
                p.op("vector", "tensor_tensor",
                     x2st_sb[:, dout * TC:(dout + 1) * TC], ot, xr, op=OP.add,
                     waits=maxw((s_dve, s_dve.n), xr_fill, x2st_guard[dout]), inc=s_dve)
                ot_free[0] = [(s_dve, s_dve.n)]
                xrel = p.drain("vector", s_dve)
                xra_free[rs] = [xrel]
                x2st_fill[dout] = xrel
            attnN_guard = (s_pe, s_pe.n)
            p.op("scalar", "dma_start", waits=maxw(*x2st_fill),
                 inc=out_sem, inc_k=16,
                 out=x_out.rearrange("(k p) t -> p k t", p=128)[:, :, sl * TC:(sl + 1) * TC],
                 in_=x2st_sb[:, :].rearrange("p (k t) -> p k t", k=DK))
            x2_store[sl] = (out_sem, out_sem.n)
            x2_store_pair = (out_sem, out_sem.n)

            lf3 = emit_ln(x2st_sb[:, :], maxw(*x2st_fill), ln3_sb[:, :], True, sl,
                          lnp, extra_p2_wait=ln3_guard)
            for dk in range(DK):
                x2st_guard[dk] = [ln_guard["pass2"], x2_store_pair]
            ln3rel = p.drain("vector", s_dve)
            p.op("scalar", "dma_start", waits=maxw(lf3, ln3rel),
                 inc=s_scq, inc_k=16,
                 out=ln3d.rearrange("(c k p) t -> p c k t", c=NTC, p=128)[:, sl],
                 in_=ln3_sb[:, :].rearrange("p (k t) -> p k t", k=DK))
            ln3_guard = [(s_pe, s_pe.n), (s_scq, s_scq.n)]
        for t in reversed(scope):
            t.__exit__(None, None, None)
        end = maxw((s_pe, s_pe.n), (s_dve, s_dve.n), (s_act, s_act.n),
                   (s_syq, s_syq.n), (s_scq, s_scq.n), (s_cc, s_cc.n),
                   (out_sem, out_sem.n))
        return x2_store, end

    fence0 = maxw(*pre_w, (s_pe, s_pe.n), (s_dve, s_dve.n))
    if debug_stop == "xraw":
        for dk in range(DK):
            p.op("sync", "dma_start", inc=s_out, inc_k=16,
                 out=out_x3.rearrange("(k p) t -> p k t", p=128)[:, dk],
                 in_=ap["x0"].rearrange("(k p) t -> p k t", p=128)[:, dk])
    elif debug_stop in ("ffn1", "ln1", "stats"):
        emit_ffn(0, ap["x0"], out_x3, fence0, None)
    elif debug_stop in ("attn", "den", "vdump", "ndump"):
        store1, end1 = emit_ffn(0, ap["x0"], x1d, fence0, None)
        emit_attn(x1d, out_x3, end1, store1)
    else:
        store1, end1 = emit_ffn(0, ap["x0"], x1d, fence0, None)
        x2s, end2 = emit_attn(x1d, x2d, end1, store1)
        emit_ffn(1, x2d, out_x3, end2, x2s)
    p.wait("sync", (s_out, s_out.n))

    with nc.Block() as block:
        for eng in ENGS:
            fns = p.q[eng]
            if not fns:
                continue
            def body(e, _fns=tuple(fns)):
                for f in _fns:
                    f(e)
            getattr(block, eng)(body)
    return nc


_BUILD_CACHE = {}


def kernel(**inputs):
    in_maps, tok_idx = _host_prepare(inputs)
    key = (_DEBUG_STOP, _PARANOID)
    if key not in _BUILD_CACHE:
        _BUILD_CACHE[key] = _build(_DEBUG_STOP)
    nc = _BUILD_CACHE[key]
    res = run_bass_kernel_spmd(nc, in_maps, core_ids=list(range(8)))
    out = np.zeros((4, 4096, 2048), dtype=np.float32)
    for core in range(8):
        b, idx = tok_idx[core]
        out[b, idx, :] = np.asarray(res.results[core]["x3"]).T
    return out
